# version 02
# baseline (speedup 1.0000x reference)
import contextlib
import numpy as np
import ml_dtypes
import concourse.bass as bass
import concourse.mybir as mybir
from concourse.bass_utils import run_bass_kernel_spmd

BF16 = ml_dtypes.bfloat16
F32 = mybir.dt.float32
BF = mybir.dt.bfloat16
I32 = mybir.dt.int32
AF = mybir.ActivationFunctionType
ALU = mybir.AluOpType
AX = mybir.AxisListType

NCORES = 8
T = 1024
D = 4096
L = 4096
NF = 2049
FP = 2176
NFC = FP // 128
HID = 11008
HIDP = 11264
HCH = HIDP // 128
HSL = HIDP // NCORES
HCSL = HSL // 128
DCN = D // 128
TCN = T // 128
EPS_LN = 1e-5
EPS_B = 1e-6

NF4 = [-1.0, -0.6961928009986877, -0.5250730514526367, -0.39491748809814453,
       -0.28444138169288635, -0.18477343022823334, -0.09105003625154495, 0.0,
       0.07958029955625534, 0.16093020141124725, 0.24611230194568634,
       0.33791524171829224, 0.44070982933044434, 0.5626170039176941,
       0.7229568362236023, 1.0]
NF4_T0 = float(np.float32(NF4[0]))
NF4_D = [float(np.float32(np.float32(NF4[v]) - np.float32(NF4[v - 1])))
         for v in range(1, 16)]

N_SUB = 4
DEBUG = False
ATTNA = True
AA_STAGE = 4
AA_NQ = 4
AA_NOTAILMM = False
PREP = True
RACECHECK = True

_CONST = {}


def const_tables():
    if _CONST:
        return _CONST
    j = np.arange(L, dtype=np.int64)[:, None]
    f = np.arange(FP, dtype=np.int64)[None, :]
    th = 2.0 * np.pi * ((j * f) % L) / L
    cf = np.cos(th)
    sf = np.sin(th)
    cf[:, NF:] = 0.0
    sf[:, NF:] = 0.0
    tabA = np.concatenate([cf, sf], axis=0).astype(BF16)
    w = np.full(FP, 2.0 / L)
    w[0] = 1.0 / L
    w[L // 2] = 1.0 / L
    w[NF:] = 0.0
    dcos = (cf * w[None, :]).T.copy()
    dsin = (sf * w[None, :]).T.copy()
    tabD = np.concatenate([dcos, dsin], axis=0).astype(BF16)
    ident = np.eye(128, dtype=BF16)
    _CONST.update(tabA=tabA, tabD=tabD, ident=ident)
    return _CONST


class Ctr:
    __slots__ = ("v",)

    def __init__(self):
        self.v = 0

    def inc(self, n=1):
        self.v += n
        return self.v


_UNIQ = [0]


def uname(base):
    _UNIQ[0] += 1
    return f"{base}_{_UNIQ[0]}"


class St:
    pass


SEM_NAMES = ("d_sy", "d_sc", "d_gp", "mm", "ve", "se", "cc")

_PASSTHRU = {"wait_ge", "sem_inc", "dma_start", "wait_op"}


def _eng_wrap(eng, s, semname):
    sem = s.sem[semname]
    ctr = s.ctr[semname]

    class _W:
        def __getattr__(self, name):
            fn = getattr(eng, name)
            if name in _PASSTHRU or not callable(fn):
                return fn

            def wrap(*a, **k):
                eng.wait_ge(sem, ctr.v)
                inst = fn(*a, **k)
                inst.then_inc(sem, 1)
                ctr.inc()
                return inst
            return wrap
    return _W()


def barrier(blk, s):
    vals = [(s.sem[k], s.ctr[k].v) for k in SEM_NAMES if s.ctr[k].v > 0]
    for en in ("sync", "scalar", "vector", "tensor", "gpsimd"):
        dec = getattr(blk, en)

        @dec
        def _(eng, vals=vals):
            for sem, v in vals:
                eng.wait_ge(sem, v)


def dma(blk, s, queue, out, in_, wait=(), **kw):
    sem = s.sem["d_" + queue]
    ctr = s.ctr["d_" + queue]
    eng = {"sy": blk.sync, "sc": blk.scalar, "gp": blk.gpsimd}[queue]

    @eng
    def _(e):
        for sn, v in wait:
            if v > 0:
                e.wait_ge(s.sem[sn], v)
        e.dma_start(out=out, in_=in_, **kw).then_inc(sem, 16)
    return ctr.inc(16)


def p_prep(nc, blk, s):
    ld_id = dma(blk, s, "sy", s.ident_sb[:], s.in_ident)
    s.ld_id = ld_id

    with contextlib.ExitStack() as cx:
        mtmp = cx.enter_context(nc.sbuf_tensor(uname("pp_mtmp"), [128, 1024], F32))
        ztmp = cx.enter_context(nc.sbuf_tensor(uname("pp_ztmp"), [128, 8], F32))
        for k in range(8):
            ld_m = dma(blk, s, "sy", mtmp[:, k * 128:(k + 1) * 128], s.in_m[k])
        for k in range(8):
            ld_z = dma(blk, s, "sy", ztmp[:, k:k + 1], s.in_z[k:k + 1, :].rearrange("o d -> d o"))

        @blk.vector
        def _(v):
            v = _eng_wrap(v, s, "ve")
            v.wait_ge(s.sem["d_sy"], ld_z)
            v.tensor_copy(s.mb_sb[:], mtmp[:])
            v.tensor_copy(s.zb_sb[:], ztmp[:])

        if not PREP:
            return
        wtile = cx.enter_context(nc.sbuf_tensor(uname("pp_wtile"), [128, 4096], F32))
        wtb = cx.enter_context(nc.sbuf_tensor(uname("pp_wtb"), [128, 4096], BF))
        strip = cx.enter_context(nc.sbuf_tensor(uname("pp_strip"), [128, 4096], BF))
        pt = cx.enter_context(nc.psum_tensor(uname("pp_pt"), [128, 512], F32))
        ptb = pt[:, 0:64].bitcast(BF)

        for qc in range(8):
            ld = dma(blk, s, "sc", wtile[:], s.in_wq[qc * 128:(qc + 1) * 128, :],
                     wait=[("mm", s.ctr["mm"].v)])

            @blk.vector
            def _(v, ld=ld):
                v = _eng_wrap(v, s, "ve")
                v.wait_ge(s.sem["d_sc"], ld)
                v.tensor_copy(wtb[:], wtile[:])
            ve_cast = s.ctr["ve"].v

            for dc in range(DCN):
                @blk.tensor
                def _(t, dc=dc, ve_cast=ve_cast):
                    t.wait_ge(s.sem["ve"], ve_cast)
                    t.wait_ge(s.sem["d_sy"], s.ld_id)
                    t.wait_ge(s.sem["ve"], s.ctr["ve"].v)
                    t.transpose(ptb, wtb[:, dc * 128:(dc + 1) * 128],
                                s.ident_sb[:]).then_inc(s.sem["mm"], 1)
                mm_tr = s.ctr["mm"].inc()

                @blk.vector
                def _(v, dc=dc, mm_tr=mm_tr):
                    v = _eng_wrap(v, s, "ve")
                    v.wait_ge(s.sem["mm"], mm_tr)
                    v.tensor_copy(strip[:, dc * 128:(dc + 1) * 128],
                                  ptb)
            dma(blk, s, "sy",
                s.wqt[:, qc * 128:(qc + 1) * 128].rearrange("(dc p) q -> p dc q", p=128),
                strip[:].rearrange("p (dc q) -> p dc q", q=128),
                wait=[("ve", s.ctr["ve"].v)])

        for dc in range(DCN):
            for k in range(8):
                ld = dma(blk, s, "sc", wtile[:, 0:512],
                         s.in_wo[dc * 128:(dc + 1) * 128, k * 512:(k + 1) * 512],
                         wait=[("mm", s.ctr["mm"].v)])

                @blk.vector
                def _(v, ld=ld):
                    v = _eng_wrap(v, s, "ve")
                    v.wait_ge(s.sem["d_sc"], ld)
                    v.wait_ge(s.sem["d_sy"], s.ctr["d_sy"].v)
                    v.tensor_tensor(wtile[:, 512:640], wtile[:, 0:128],
                                    wtile[:, 128:256], ALU.add)
                    v.tensor_tensor(wtile[:, 640:768], wtile[:, 256:384],
                                    wtile[:, 384:512], ALU.add)
                    v.tensor_tensor(wtile[:, 512:640], wtile[:, 512:640],
                                    wtile[:, 640:768], ALU.add)
                    v.tensor_copy(wtb[:, 0:128], wtile[:, 512:640])
                ve_fold = s.ctr["ve"].v

                @blk.tensor
                def _(t, ve_fold=ve_fold):
                    t.wait_ge(s.sem["ve"], ve_fold)
                    t.wait_ge(s.sem["d_sy"], s.ld_id)
                    t.transpose(ptb, wtb[:, 0:128], s.ident_sb[:]).then_inc(s.sem["mm"], 1)
                mm_tr = s.ctr["mm"].inc()

                @blk.vector
                def _(v, mm_tr=mm_tr):
                    v = _eng_wrap(v, s, "ve")
                    v.wait_ge(s.sem["mm"], mm_tr)
                    v.tensor_copy(strip[:, 0:128], ptb)
                dma(blk, s, "sy",
                    s.woft[k * 128:(k + 1) * 128, dc * 128:(dc + 1) * 128],
                    strip[:, 0:128], wait=[("ve", s.ctr["ve"].v)])

        qtile = cx.enter_context(nc.sbuf_tensor(uname("dq_q"), [128, 4096], I32))
        qf = cx.enter_context(nc.sbuf_tensor(uname("dq_qf"), [128, 4096], F32))
        acc = cx.enter_context(nc.sbuf_tensor(uname("dq_acc"), [128, 4096], F32))
        tmp = cx.enter_context(nc.sbuf_tensor(uname("dq_tmp"), [128, 4096], F32))
        wbb = cx.enter_context(nc.sbuf_tensor(uname("dq_wb"), [128, 4096], BF))
        sct = cx.enter_context(nc.sbuf_tensor(uname("dq_sc"), [128, 64], F32))

        def dequant_into_wb(qsrc, scsrc, nblk, width):
            ld_q = dma(blk, s, "sc", qtile[:, 0:width], qsrc,
                       wait=[("ve", s.ctr["ve"].v)])
            ld_s = dma(blk, s, "sc", sct[:, 0:nblk], scsrc)

            @blk.vector
            def _(v):
                v = _eng_wrap(v, s, "ve")
                v.wait_ge(s.sem["d_sc"], ld_s)
                v.wait_ge(s.sem["mm"], s.ctr["mm"].v)
                v.tensor_copy(qf[:, 0:width], qtile[:, 0:width])
                v.tensor_scalar(acc[:, 0:width], qf[:, 0:width], 1.0, NF4_D[0],
                                ALU.is_ge, ALU.mult)
                for vv in range(2, 16):
                    v.tensor_scalar(tmp[:, 0:width], qf[:, 0:width], float(vv),
                                    NF4_D[vv - 1], ALU.is_ge, ALU.mult)
                    v.tensor_tensor(acc[:, 0:width], acc[:, 0:width],
                                    tmp[:, 0:width], ALU.add)
                for b in range(nblk):
                    v.tensor_scalar(wbb[:, b * 64:(b + 1) * 64],
                                    acc[:, b * 64:(b + 1) * 64],
                                    NF4_T0, sct[:, b:b + 1], ALU.add, ALU.mult)
                v.tensor_copy(tmp[:, 0:1], acc[:, 0:1])
            return s.ctr["ve"].v

        def transpose_store(ve_w, nblocks, dst_ap_fn):
            for bi in range(nblocks):
                @blk.tensor
                def _(t, bi=bi, ve_w=ve_w):
                    t.wait_ge(s.sem["ve"], ve_w)
                    t.wait_ge(s.sem["d_sy"], s.ld_id)
                    t.wait_ge(s.sem["ve"], s.ctr["ve"].v)
                    if bi == 0:
                        t.wait_ge(s.sem["d_sy"], s.ctr["d_sy"].v)
                    t.transpose(ptb, wbb[:, bi * 128:(bi + 1) * 128],
                                s.ident_sb[:]).then_inc(s.sem["mm"], 1)
                mm_tr = s.ctr["mm"].inc()

                @blk.vector
                def _(v, bi=bi, mm_tr=mm_tr):
                    v = _eng_wrap(v, s, "ve")
                    v.wait_ge(s.sem["mm"], mm_tr)
                    v.tensor_copy(strip[:, bi * 128:(bi + 1) * 128],
                                  ptb)
            dram_ap, sbuf_ap = dst_ap_fn()
            dma(blk, s, "sy", dram_ap, sbuf_ap, wait=[("ve", s.ctr["ve"].v)])

        for (qarr, scarr, dstT) in ((s.in_w1q, s.in_w1s, s.w1t_my),
                                    (s.in_w2q, s.in_w2s, s.w2t_my)):
            for hl in range(HCSL):
                ve_w = dequant_into_wb(qarr[hl * 128:(hl + 1) * 128, :],
                                       scarr[hl * 128:(hl + 1) * 128, :], 64, 4096)
                transpose_store(
                    ve_w, DCN,
                    lambda dstT=dstT, hl=hl: (
                        dstT[:, hl * 128:(hl + 1) * 128].rearrange(
                            "(dc p) h -> p dc h", p=128),
                        strip[:].rearrange("p (dc h) -> p dc h", h=128)))

        for dc3 in range(DCN):
            ve_w = dequant_into_wb(s.in_w3q[dc3 * 128:(dc3 + 1) * 128, :],
                                   s.in_w3s[dc3 * 128:(dc3 + 1) * 128, :], 22, HSL)
            transpose_store(
                ve_w, HCSL,
                lambda dc3=dc3: (
                    s.w3t_my[:, dc3 * 128:(dc3 + 1) * 128].rearrange(
                        "(hl p) d -> p hl d", p=128),
                    strip[:, 0:HSL].rearrange("p (hl d) -> p hl d", d=128)))

    @blk.gpsimd
    def _(g):
        g.wait_ge(s.sem["d_sy"], s.ctr["d_sy"].v)
        g.wait_ge(s.sem["d_sc"], s.ctr["d_sc"].v)
        for piece, full in ((s.w1t_my_h, s.w1t_g_h), (s.w2t_my_h, s.w2t_g_h),
                            (s.w3t_my_h, s.w3t_g_h)):
            g.collective_compute(
                "AllGather", ALU.bypass,
                replica_groups=[list(range(NCORES))],
                ins=[piece.ap().opt()],
                outs=[full.ap().opt()],
            ).then_inc(s.sem["cc"])
    s.ctr["cc"].inc(3)


def p_ln(nc, blk, s, src_ap, to_dram, save_coefs=False):
    with contextlib.ExitStack() as cx:
        xt = cx.enter_context(nc.sbuf_tensor(uname("ln_x"), [128, 4096], F32))
        dum = cx.enter_context(nc.sbuf_tensor(uname("ln_dum"), [128, 4096], BF))
        xn = cx.enter_context(nc.sbuf_tensor(uname("ln_xn"), [128, 4096], BF))
        strip = cx.enter_context(nc.sbuf_tensor(uname("ln_strip"), [128, 4096], BF))
        cols = cx.enter_context(nc.sbuf_tensor(uname("ln_cols"), [128, 6], F32))
        ptl = cx.enter_context(nc.psum_tensor(uname("ln_pt"), [128, 512], F32))
        ptlb = ptl[:, 0:64].bitcast(BF)
        scol, qcol, mcol, vcol, rcol, bcol = (cols[:, i:i + 1] for i in range(6))

        for tc in range(TCN):
            ld = dma(blk, s, "sc", xt[:], src_ap[tc * 128:(tc + 1) * 128, :],
                     wait=[("ve", s.ctr["ve"].v), ("se", s.ctr["se"].v)])

            @blk.vector
            def _(v, ld=ld):
                v = _eng_wrap(v, s, "ve")
                v.wait_ge(s.sem["d_sc"], ld)
                v.reduce_sum(scol, xt[:], axis=AX.X)
            ve_sum = s.ctr["ve"].v

            @blk.scalar
            def _(e, ld=ld):
                e = _eng_wrap(e, s, "se")
                e.wait_ge(s.sem["d_sc"], ld)
                e.activation(dum[:], xt[:], AF.Square, bias=0.0, scale=1.0,
                             accum_out=qcol)
            se_sq = s.ctr["se"].v

            @blk.vector
            def _(v, se_sq=se_sq, ve_sum=ve_sum):
                v = _eng_wrap(v, s, "ve")
                v.wait_ge(s.sem["se"], se_sq)
                v.wait_ge(s.sem["ve"], ve_sum)
                v.tensor_scalar(mcol, scol, 1.0 / D, None, ALU.mult)
                v.tensor_scalar(vcol, qcol, 1.0 / D, None, ALU.mult)
                v.tensor_tensor(bcol, mcol, mcol, ALU.mult)
                v.tensor_tensor(vcol, vcol, bcol, ALU.subtract)
                v.tensor_scalar(vcol, vcol, float(EPS_LN), None,
                                ALU.add)
            ve_var = s.ctr["ve"].v

            @blk.scalar
            def _(e, ve_var=ve_var):
                e = _eng_wrap(e, s, "se")
                e.wait_ge(s.sem["ve"], ve_var)
                e.activation(rcol, vcol, AF.Sqrt, bias=0.0, scale=1.0)
            se_rt = s.ctr["se"].v

            @blk.vector
            def _(v, se_rt=se_rt):
                v = _eng_wrap(v, s, "ve")
                v.wait_ge(s.sem["se"], se_rt)
                v.wait_ge(s.sem["mm"], s.ctr["mm"].v)
                v.reciprocal(rcol, rcol)
            ve_rc = s.ctr["ve"].v

            @blk.vector
            def _(v, tc=tc, ve_rc=ve_rc):
                v = _eng_wrap(v, s, "ve")
                v.wait_ge(s.sem["ve"], ve_rc)
                v.tensor_tensor(bcol, mcol, rcol, ALU.mult)
                v.tensor_scalar(bcol, bcol, -1.0, None, ALU.mult)
                if save_coefs:
                    v.tensor_copy(s.lncoef[:, 2 * tc:2 * tc + 1], rcol)
                    v.tensor_copy(s.lncoef[:, 2 * tc + 1:2 * tc + 2], bcol)
                v.tensor_copy(qcol, bcol)
            ve_bc = s.ctr["ve"].v

            @blk.vector
            def _(v, ve_bc=ve_bc):
                v = _eng_wrap(v, s, "ve")
                v.wait_ge(s.sem["ve"], ve_bc)
                v.tensor_scalar(xn[:], xt[:], rcol, bcol, ALU.mult,
                                ALU.add)
            ve_xn = s.ctr["ve"].v

            for dc in range(DCN):
                @blk.tensor
                def _(t, dc=dc, ve_xn=ve_xn):
                    t.wait_ge(s.sem["ve"], ve_xn)
                    t.wait_ge(s.sem["ve"], s.ctr["ve"].v)
                    t.wait_ge(s.sem["d_sy"], s.ld_id)
                    if dc == 0:
                        t.wait_ge(s.sem["d_sy"], s.ctr["d_sy"].v)
                    t.transpose(ptlb, xn[:, dc * 128:(dc + 1) * 128],
                                s.ident_sb[:]).then_inc(s.sem["mm"], 1)
                mm_tr = s.ctr["mm"].inc()

                @blk.vector
                def _(v, dc=dc, tc=tc, mm_tr=mm_tr):
                    v = _eng_wrap(v, s, "ve")
                    v.wait_ge(s.sem["mm"], mm_tr)
                    if to_dram:
                        v.tensor_copy(strip[:, dc * 128:(dc + 1) * 128],
                                      ptlb)
                    else:
                        v.tensor_copy(
                            s.xt_sb[:, dc * 1024 + tc * 128: dc * 1024 + (tc + 1) * 128],
                            ptlb)

            if to_dram:
                dma(blk, s, "sy",
                    s.lnx[:, tc * 128:(tc + 1) * 128].rearrange("(dc p) t -> p dc t", p=128),
                    strip[:].rearrange("p (dc t) -> p dc t", t=128),
                    wait=[("ve", s.ctr["ve"].v)])


def p_attn_a(nc, blk, s, src_ap, dst_ap):
    QW = 256
    NQ = T // QW
    TPQ = QW // 128
    for qq in range(min(NQ, AA_NQ)):
        with contextlib.ExitStack() as cx:
            lnh = cx.enter_context(nc.sbuf_tensor(uname("aa_lnh"), [128, DCN * QW], BF))
            reb = cx.enter_context(nc.sbuf_tensor(uname("aa_re"), [128, NFC * QW], BF))
            imb = cx.enter_context(nc.sbuf_tensor(uname("aa_im"), [128, NFC * QW], BF))
            pbf = cx.enter_context(nc.sbuf_tensor(uname("aa_p"), [128, NFC * QW], BF))
            slab = cx.enter_context(nc.sbuf_tensor(uname("aa_slab"), [128, 4096], BF))
            slab2 = cx.enter_context(nc.sbuf_tensor(uname("aa_slab2"), [128, 4096], BF))
            corr = cx.enter_context(nc.sbuf_tensor(uname("aa_corr"), [128, 4096], F32))
            ebuf = cx.enter_context(nc.sbuf_tensor(uname("aa_e"), [128, 4096], F32))
            mbuf = cx.enter_context(nc.sbuf_tensor(uname("aa_m"), [128, 4096], F32))
            gbuf = cx.enter_context(nc.sbuf_tensor(uname("aa_g"), [128, 4096], BF))
            gt = cx.enter_context(nc.sbuf_tensor(uname("aa_gt"), [128, DCN * QW], BF))
            tre = cx.enter_context(nc.sbuf_tensor(uname("aa_tre"), [128, NFC * QW], BF))
            tim = cx.enter_context(nc.sbuf_tensor(uname("aa_tim"), [128, NFC * QW], BF))
            gc2 = cx.enter_context(nc.sbuf_tensor(uname("aa_gc"), [128, QW], BF))
            gs2 = cx.enter_context(nc.sbuf_tensor(uname("aa_gs"), [128, QW], BF))
            tt1 = cx.enter_context(nc.sbuf_tensor(uname("aa_tt1"), [128, QW], F32))
            tt2 = cx.enter_context(nc.sbuf_tensor(uname("aa_tt2"), [128, QW], F32))
            t8 = cx.enter_context(nc.sbuf_tensor(uname("aa_t8"), [128, 8], F32))
            e8 = cx.enter_context(nc.sbuf_tensor(uname("aa_e8"), [128, 8], F32))
            ccols = cx.enter_context(nc.sbuf_tensor(uname("aa_cols"), [128, 4], F32))
            xtl = cx.enter_context(nc.sbuf_tensor(uname("aa_x"), [128, 4096], F32))
            ps8 = [cx.enter_context(nc.psum_tensor(uname(f"aa_ps{i}"), [128, 512], F32))
                   for i in range(8)]
            psb = ps8[0][:, 0:64].bitcast(BF)
            nmx, zc, c1c, c2c = (ccols[:, i:i + 1] for i in range(4))

            ld_ln = dma(blk, s, "sc",
                        lnh[:].rearrange("p (dc t) -> p dc t", t=QW),
                        s.lnx[:, qq * QW:(qq + 1) * QW].rearrange(
                            "(dc p) t -> p dc t", p=128))

            for fc in range(NFC):
                w_mm = [("mm", s.ctr["mm"].v)]
                ld_c = dma(blk, s, "sy",
                           slab[:].rearrange("p (jc f) -> p jc f", f=128),
                           s.in_tabA[0:4096, fc * 128:(fc + 1) * 128].rearrange(
                               "(jc p) f -> p jc f", p=128), wait=w_mm)
                ld_s2 = dma(blk, s, "sy",
                            slab2[:].rearrange("p (jc f) -> p jc f", f=128),
                            s.in_tabA[4096:8192, fc * 128:(fc + 1) * 128].rearrange(
                                "(jc p) f -> p jc f", p=128))

                @blk.tensor
                def _(t, fc=fc, ld_s2=ld_s2, ld_ln=ld_ln):
                    t.wait_ge(s.sem["d_sy"], ld_s2)
                    t.wait_ge(s.sem["d_sc"], ld_ln)
                    t.wait_ge(s.sem["ve"], s.ctr["ve"].v)
                    for dc in range(DCN):
                        t.matmul(ps8[0][:, 0:QW], slab[:, dc * 128:(dc + 1) * 128],
                                 lnh[:, dc * QW:(dc + 1) * QW],
                                 start=(dc == 0), stop=(dc == DCN - 1))
                    for dc in range(DCN):
                        inst = t.matmul(ps8[1][:, 0:QW], slab2[:, dc * 128:(dc + 1) * 128],
                                        lnh[:, dc * QW:(dc + 1) * QW],
                                        start=(dc == 0), stop=(dc == DCN - 1))
                    inst.then_inc(s.sem["mm"], 1)
                mm_f = s.ctr["mm"].inc()

                @blk.vector
                def _(v, fc=fc, mm_f=mm_f):
                    v = _eng_wrap(v, s, "ve")
                    v.wait_ge(s.sem["mm"], mm_f)
                    sl = slice(fc * QW, (fc + 1) * QW)
                    v.tensor_copy(reb[:, sl], ps8[0][:, 0:QW])
                    v.tensor_copy(imb[:, sl], ps8[1][:, 0:QW])
                    v.tensor_tensor(tt1[:], reb[:, sl], reb[:, sl], ALU.mult)
                    v.tensor_tensor(tt2[:], imb[:, sl], imb[:, sl], ALU.mult)
                    v.tensor_tensor(pbf[:, sl], tt1[:], tt2[:],
                                    ALU.add)

            if DEBUG and qq == 0:
                barrier(blk, s)
                dma(blk, s, "sy", s.dbg["dbg_re"], reb[:])
                dma(blk, s, "sy", s.dbg["dbg_im"], imb[:])
                dma(blk, s, "sy", s.dbg["dbg_pb"], pbf[:])

            for ti in range(TPQ if AA_STAGE >= 2 else 0):
                tc = qq * TPQ + ti
                for fc in range(NFC):
                    ld_d = dma(blk, s, "sy", slab[:],
                               s.in_tabD[fc * 128:(fc + 1) * 128, :],
                               wait=[("mm", s.ctr["mm"].v)])

                    @blk.tensor
                    def _(t, fc=fc, ti=ti, ld_d=ld_d):
                        t.wait_ge(s.sem["d_sy"], ld_d)
                        if fc == 0:
                            t.wait_ge(s.sem["ve"], s.ctr["ve"].v)
                        for ts_ in range(8):
                            inst = t.matmul(ps8[ts_][:],
                                            pbf[:, fc * QW + ti * 128: fc * QW + (ti + 1) * 128],
                                            slab[:, ts_ * 512:(ts_ + 1) * 512],
                                            start=(fc == 0), stop=(fc == NFC - 1))
                        inst.then_inc(s.sem["mm"], 1)
                    s.ctr["mm"].inc()
                mm_corr = s.ctr["mm"].v

                @blk.vector
                def _(v, mm_corr=mm_corr):
                    v = _eng_wrap(v, s, "ve")
                    v.wait_ge(s.sem["mm"], mm_corr)
                    for ts_ in range(8):
                        inst = v.tensor_copy(corr[:, ts_ * 512:(ts_ + 1) * 512],
                                             ps8[ts_][:])
                    inst
                ve_cp = s.ctr["ve"].v

                @blk.vector
                def _(v, ve_cp=ve_cp):
                    v = _eng_wrap(v, s, "ve")
                    v.wait_ge(s.sem["ve"], ve_cp)
                    v.max(t8[:], corr[:])
                ve_mx = s.ctr["ve"].v

                @blk.vector
                def _(v, ve_mx=ve_mx):
                    v = _eng_wrap(v, s, "ve")
                    v.wait_ge(s.sem["ve"], ve_mx)
                    v.tensor_scalar(nmx, t8[:, 0:1], -1.0, None,
                                    ALU.mult)
                ve_c = s.ctr["ve"].v

                @blk.scalar
                def _(e, ve_c=ve_c):
                    e = _eng_wrap(e, s, "se")
                    e.wait_ge(s.sem["ve"], ve_c)
                    e.activation(e8[:], t8[:], AF.Exp, bias=nmx, scale=1.0)
                    e.activation(ebuf[:], corr[:], AF.Exp, bias=nmx,
                                 scale=1.0)
                se_e = s.ctr["se"].v

                @blk.vector
                def _(v, se_e=se_e):
                    v = _eng_wrap(v, s, "ve")
                    v.wait_ge(s.sem["se"], se_e)
                    v.reduce_sum(zc, e8[:], axis=AX.X)
                ve_rs = s.ctr["ve"].v

                @blk.vector
                def _(v, ve_rs=ve_rs):
                    v = _eng_wrap(v, s, "ve")
                    v.wait_ge(s.sem["ve"], ve_rs)
                    v.reciprocal(zc, zc)
                ve_rz = s.ctr["ve"].v

                @blk.vector
                def _(v, tc=tc, ve_rz=ve_rz):
                    v = _eng_wrap(v, s, "ve")
                    v.wait_ge(s.sem["ve"], ve_rz)
                    v.tensor_copy(s.rz8[:, tc:tc + 1], zc)
                    v.tensor_scalar(mbuf[:], corr[:], t8[:, 7:8], None, ALU.is_ge)
                    v.tensor_tensor(ebuf[:], mbuf[:], ebuf[:], ALU.mult)
                    v.tensor_scalar(ebuf[:, 0:1], ebuf[:, 0:1], 0.0, None, ALU.mult)
                    v.tensor_scalar(gbuf[:], ebuf[:], zc, None,
                                    ALU.mult)
                ve_g = s.ctr["ve"].v

                for dcq in range(DCN):
                    @blk.tensor
                    def _(t, dcq=dcq, ve_g=ve_g):
                        t.wait_ge(s.sem["ve"], ve_g)
                        t.wait_ge(s.sem["ve"], s.ctr["ve"].v)
                        t.transpose(psb, gbuf[:, dcq * 128:(dcq + 1) * 128],
                                    s.ident_sb[:]).then_inc(s.sem["mm"], 1)
                    mm_tr = s.ctr["mm"].inc()

                    @blk.vector
                    def _(v, dcq=dcq, ti=ti, mm_tr=mm_tr):
                        v = _eng_wrap(v, s, "ve")
                        v.wait_ge(s.sem["mm"], mm_tr)
                        v.tensor_copy(gt[:, dcq * QW + ti * 128: dcq * QW + (ti + 1) * 128],
                                      psb)

            if DEBUG and qq == 0 and AA_STAGE >= 2:
                barrier(blk, s)
                dma(blk, s, "sy", s.dbg["dbg_corr"], corr[:])
                dma(blk, s, "sy", s.dbg["dbg_t8"], t8[:])
                dma(blk, s, "sy", s.dbg["dbg_g"], gbuf[:])

            for fc in range(NFC if AA_STAGE >= 3 else 0):
                w_mm = [("mm", s.ctr["mm"].v)]
                ld_c = dma(blk, s, "sy",
                           slab[:].rearrange("p (jc f) -> p jc f", f=128),
                           s.in_tabA[0:4096, fc * 128:(fc + 1) * 128].rearrange(
                               "(jc p) f -> p jc f", p=128), wait=w_mm)
                ld_s2 = dma(blk, s, "sy",
                            slab2[:].rearrange("p (jc f) -> p jc f", f=128),
                            s.in_tabA[4096:8192, fc * 128:(fc + 1) * 128].rearrange(
                                "(jc p) f -> p jc f", p=128))

                @blk.tensor
                def _(t, fc=fc, ld_s2=ld_s2):
                    t.wait_ge(s.sem["d_sy"], ld_s2)
                    t.wait_ge(s.sem["ve"], s.ctr["ve"].v)
                    for tq in range(DCN):
                        t.matmul(ps8[0][:, 0:QW], slab[:, tq * 128:(tq + 1) * 128],
                                 gt[:, tq * QW:(tq + 1) * QW],
                                 start=(tq == 0), stop=(tq == DCN - 1))
                    for tq in range(DCN):
                        inst = t.matmul(ps8[1][:, 0:QW], slab2[:, tq * 128:(tq + 1) * 128],
                                        gt[:, tq * QW:(tq + 1) * QW],
                                        start=(tq == 0), stop=(tq == DCN - 1))
                    inst.then_inc(s.sem["mm"], 1)
                mm_g = s.ctr["mm"].inc()

                @blk.vector
                def _(v, fc=fc, mm_g=mm_g):
                    v = _eng_wrap(v, s, "ve")
                    v.wait_ge(s.sem["mm"], mm_g)
                    sl = slice(fc * QW, (fc + 1) * QW)
                    v.tensor_copy(gc2[:], ps8[0][:, 0:QW])
                    v.tensor_copy(gs2[:], ps8[1][:, 0:QW])
                    v.tensor_tensor(tt1[:], reb[:, sl], gc2[:], ALU.mult)
                    v.tensor_tensor(tt2[:], imb[:, sl], gs2[:], ALU.mult)
                    v.tensor_tensor(tre[:, sl], tt1[:], tt2[:], ALU.add)
                    v.tensor_tensor(tt1[:], imb[:, sl], gc2[:], ALU.mult)
                    v.tensor_tensor(tt2[:], reb[:, sl], gs2[:], ALU.mult)
                    v.tensor_tensor(tim[:, sl], tt1[:], tt2[:],
                                    ALU.subtract)

            if DEBUG and qq == 0 and AA_STAGE >= 3:
                barrier(blk, s)
                dma(blk, s, "sy", s.dbg["dbg_tre"], tre[:])
                dma(blk, s, "sy", s.dbg["dbg_tim"], tim[:])

            for ti in range(TPQ if AA_STAGE >= 4 else 0):
                tc = qq * TPQ + ti
                ld_x = dma(blk, s, "sc", xtl[:], src_ap[tc * 128:(tc + 1) * 128, :],
                           wait=[("d_sy", s.ctr["d_sy"].v)])

                @blk.vector
                def _(v, tc=tc, qq=qq, ti=ti):
                    v = _eng_wrap(v, s, "ve")
                    v.tensor_tensor(c1c, s.lncoef[:, 2 * tc:2 * tc + 1],
                                    s.rz8[:, tc:tc + 1], ALU.mult)
                    v.tensor_scalar(c1c, c1c, 1.0, None, ALU.add)
                    v.tensor_tensor(c2c, s.lncoef[:, 2 * tc + 1:2 * tc + 2],
                                    s.rz8[:, tc:tc + 1], ALU.mult)
                    if DEBUG and qq == 0 and ti == 0:
                        v.tensor_copy(s.rz8[:, 6:7], c1c)
                        v.tensor_copy(s.rz8[:, 7:8], c2c)
                    v.tensor_copy(c2c, c2c)
                ve_cc = s.ctr["ve"].v

                @blk.vector
                def _(v, ld_x=ld_x, ve_cc=ve_cc):
                    v = _eng_wrap(v, s, "ve")
                    v.wait_ge(s.sem["d_sc"], ld_x)
                    v.wait_ge(s.sem["ve"], ve_cc)
                    v.tensor_scalar(xtl[:], xtl[:], c1c, None, ALU.mult)
                    v.tensor_scalar(xtl[:], xtl[:], c2c, None,
                                    ALU.add)

                for fc in range(NFC if not AA_NOTAILMM else 0):
                    w_mm = [("mm", s.ctr["mm"].v)]
                    ld_dc = dma(blk, s, "sy", slab[:],
                                s.in_tabD[fc * 128:(fc + 1) * 128, :], wait=w_mm)
                    ld_ds = dma(blk, s, "sy", slab2[:],
                                s.in_tabD[FP + fc * 128: FP + (fc + 1) * 128, :])

                    @blk.tensor
                    def _(t, fc=fc, ti=ti, ld_ds=ld_ds):
                        t.wait_ge(s.sem["d_sy"], ld_ds)
                        if fc == 0:
                            t.wait_ge(s.sem["ve"], s.ctr["ve"].v)
                        for ds in range(8):
                            t.matmul(ps8[ds][:],
                                     tre[:, fc * QW + ti * 128: fc * QW + (ti + 1) * 128],
                                     slab[:, ds * 512:(ds + 1) * 512],
                                     start=(fc == 0), stop=False)
                            inst = t.matmul(ps8[ds][:],
                                            tim[:, fc * QW + ti * 128: fc * QW + (ti + 1) * 128],
                                            slab2[:, ds * 512:(ds + 1) * 512],
                                            start=False, stop=(fc == NFC - 1))
                        inst.then_inc(s.sem["mm"], 1)
                    s.ctr["mm"].inc()
                mm_tail = s.ctr["mm"].v

                @blk.vector
                def _(v, mm_tail=mm_tail):
                    v = _eng_wrap(v, s, "ve")
                    v.wait_ge(s.sem["mm"], mm_tail)
                    if AA_NOTAILMM:
                        inst = v.tensor_copy(c1c, c1c)
                    else:
                        for ds in range(8):
                            inst = v.tensor_tensor(xtl[:, ds * 512:(ds + 1) * 512],
                                                   xtl[:, ds * 512:(ds + 1) * 512],
                                                   ps8[ds][:], ALU.add)
                    inst
                ve_up = s.ctr["ve"].v

                dma(blk, s, "sy", dst_ap[tc * 128:(tc + 1) * 128, :], xtl[:],
                    wait=[("ve", ve_up)])


def p_ffn(nc, blk, s, src_ap, dst_ap):
    h_done = [0] * HCH
    with contextlib.ExitStack() as cx:
        w1s = [cx.enter_context(nc.sbuf_tensor(uname(f"f1_w1{i}"), [128, 4096], BF))
               for i in range(2)]
        w2s = [cx.enter_context(nc.sbuf_tensor(uname(f"f1_w2{i}"), [128, 4096], BF))
               for i in range(2)]
        sg = cx.enter_context(nc.sbuf_tensor(uname("f1_sg"), [128, 1024], F32))
        hb = [cx.enter_context(nc.sbuf_tensor(uname(f"f1_hb{i}"), [128, 1024], BF))
              for i in range(2)]
        ps8 = [cx.enter_context(nc.psum_tensor(uname(f"f1_ps{i}"), [128, 512], F32))
               for i in range(8)]
        mm_hist = [0, 0]
        ve_hist = [0, 0]
        hstore = [0, 0]

        for hc in range(HCH):
            par = hc % 2
            r, hl = hc // HCSL, hc % HCSL
            w_w = [("mm", mm_hist[par])]
            if hc < 2:
                w_w.append(("cc", s.ctr["cc"].v))
            ld1 = dma(blk, s, "sy",
                      w1s[par][:].rearrange("p (dc h) -> p dc h", h=128),
                      s.w1t_g[r * 4096:(r + 1) * 4096, hl * 128:(hl + 1) * 128]
                      .rearrange("(dc p) h -> p dc h", p=128), wait=w_w)
            ld2 = dma(blk, s, "sy",
                      w2s[par][:].rearrange("p (dc h) -> p dc h", h=128),
                      s.w2t_g[r * 4096:(r + 1) * 4096, hl * 128:(hl + 1) * 128]
                      .rearrange("(dc p) h -> p dc h", p=128))

            @blk.tensor
            def _(t, par=par, ld2=ld2):
                t.wait_ge(s.sem["d_sy"], ld2)
                t.wait_ge(s.sem["ve"], ve_hist[par])
                for dc in range(DCN):
                    for (pg, wsl, toff) in ((0, w1s[par], 0), (1, w1s[par], 512),
                                            (2, w2s[par], 0), (3, w2s[par], 512)):
                        inst = t.matmul(ps8[par * 4 + pg][:],
                                        wsl[:, dc * 128:(dc + 1) * 128],
                                        s.xt_sb[:, dc * 1024 + toff: dc * 1024 + toff + 512],
                                        start=(dc == 0), stop=(dc == DCN - 1))
                inst.then_inc(s.sem["mm"], 1)
            mm_hist[par] = s.ctr["mm"].inc()

            @blk.scalar
            def _(e, par=par, mmv=mm_hist[par]):
                e = _eng_wrap(e, s, "se")
                e.wait_ge(s.sem["mm"], mmv)
                e.activation(sg[:, 0:512], ps8[par * 4 + 0][:], AF.Silu,
                             bias=0.0, scale=1.0)
                e.activation(sg[:, 512:1024], ps8[par * 4 + 1][:], AF.Silu,
                             bias=0.0, scale=1.0)
            se_si = s.ctr["se"].v

            @blk.vector
            def _(v, par=par, se_si=se_si, hs=hstore[par]):
                v = _eng_wrap(v, s, "ve")
                v.wait_ge(s.sem["se"], se_si)
                v.wait_ge(s.sem["d_gp"], hs)
                v.tensor_tensor(hb[par][:, 0:512], sg[:, 0:512],
                                ps8[par * 4 + 2][:], ALU.mult)
                v.tensor_tensor(hb[par][:, 512:1024], sg[:, 512:1024],
                                ps8[par * 4 + 3][:], ALU.mult)
            ve_hist[par] = s.ctr["ve"].v

            h_done[hc] = dma(blk, s, "gp", s.hprime[hc * 128:(hc + 1) * 128, :],
                             hb[par][:], wait=[("ve", ve_hist[par])])
            hstore[par] = h_done[hc]

    barrier(blk, s)

    with contextlib.ExitStack() as cx:
        w3b = [cx.enter_context(nc.sbuf_tensor(uname(f"f2_w3{i}"), [128, 512], BF))
               for i in range(2)]
        hcb = [cx.enter_context(nc.sbuf_tensor(uname(f"f2_hc{i}"), [128, 1024], BF))
               for i in range(2)]
        xsl = cx.enter_context(nc.sbuf_tensor(uname("f2_xs"), [128, 8 * 512], F32))
        ps8 = [cx.enter_context(nc.psum_tensor(uname(f"f2_ps{i}"), [128, 512], F32))
               for i in range(8)]

        mm_last = [0, 0]
        for ds in range(8):
            for hc in range(HCH):
                par = hc % 2
                w_w = [("mm", mm_last[par])]
                ld3 = dma(blk, s, "sy", w3b[par][:],
                          s.w3t_g[hc * 128:(hc + 1) * 128, ds * 512:(ds + 1) * 512],
                          wait=w_w)
                ldh = dma(blk, s, "sc", hcb[par][:],
                          s.hprime[hc * 128:(hc + 1) * 128, :], wait=w_w)

                @blk.tensor
                def _(t, par=par, hc=hc, ld3=ld3, ldh=ldh):
                    t.wait_ge(s.sem["d_sy"], ld3)
                    t.wait_ge(s.sem["d_sc"], ldh)
                    if hc == 0:
                        t.wait_ge(s.sem["ve"], s.ctr["ve"].v)
                    for tc in range(TCN):
                        inst = t.matmul(ps8[tc][:], hcb[par][:, tc * 128:(tc + 1) * 128],
                                        w3b[par][:],
                                        start=(hc == 0), stop=(hc == HCH - 1))
                    inst.then_inc(s.sem["mm"], 1)
                mm_last[par] = s.ctr["mm"].inc()
            mm_ds = s.ctr["mm"].v

            lds = []
            for tc in range(TCN):
                lds.append(dma(blk, s, "sc", xsl[:, tc * 512:(tc + 1) * 512],
                               src_ap[tc * 128:(tc + 1) * 128, ds * 512:(ds + 1) * 512],
                               wait=[("d_sy", s.ctr["d_sy"].v)] if tc == 0 else ()))

            @blk.vector
            def _(v, mm_ds=mm_ds, lds=lds):
                v = _eng_wrap(v, s, "ve")
                v.wait_ge(s.sem["mm"], mm_ds)
                v.wait_ge(s.sem["d_sc"], lds[-1])
                for tc in range(TCN):
                    inst = v.tensor_tensor(xsl[:, tc * 512:(tc + 1) * 512],
                                           xsl[:, tc * 512:(tc + 1) * 512],
                                           ps8[tc][:], ALU.add)
                inst
            ve_up = s.ctr["ve"].v

            for tc in range(TCN):
                dma(blk, s, "sy", dst_ap[tc * 128:(tc + 1) * 128, ds * 512:(ds + 1) * 512],
                    xsl[:, tc * 512:(tc + 1) * 512],
                    wait=[("ve", ve_up)] if tc == 0 else ())


def p_attn_b(nc, blk, s, src_ap, dst_ap):
    with contextlib.ExitStack() as cx:
        bqb = cx.enter_context(nc.sbuf_tensor(uname("ab_bq"), [128, 1024], F32))
        wqs = cx.enter_context(nc.sbuf_tensor(uname("ab_wqs"), [128, 512], BF))
        sig = cx.enter_context(nc.sbuf_tensor(uname("ab_sig"), [128, 8 * 1024], BF))
        sigt = cx.enter_context(nc.sbuf_tensor(uname("ab_sigt"), [128, 8 * 1024], BF))
        retb = cx.enter_context(nc.sbuf_tensor(uname("ab_ret"), [128, 8 * 1024], BF))
        rden = cx.enter_context(nc.sbuf_tensor(uname("ab_rden"), [128, 1024], F32))
        drow = cx.enter_context(nc.sbuf_tensor(uname("ab_drow"), [1, 1024], F32))
        tqa = cx.enter_context(nc.sbuf_tensor(uname("ab_tqa"), [128, 512], F32))
        tqb = cx.enter_context(nc.sbuf_tensor(uname("ab_tqb"), [128, 512], F32))
        tqc = cx.enter_context(nc.sbuf_tensor(uname("ab_tqc"), [128, 512], F32))
        wof = cx.enter_context(nc.sbuf_tensor(uname("ab_wof"), [128, 512], BF))
        xsl = cx.enter_context(nc.sbuf_tensor(uname("ab_xs"), [128, 8 * 512], F32))
        ps8 = [cx.enter_context(nc.psum_tensor(uname(f"ab_ps{i}"), [128, 512], F32))
               for i in range(8)]
        psb = ps8[0][:, 0:64].bitcast(BF)

        ld_bq = dma(blk, s, "sy", bqb[:],
                    s.in_bq.unsqueeze(0).broadcast_to((128, 1024)))

        for qs in range(2):
            for dc in range(DCN):
                ld_w = dma(blk, s, "sy", wqs[:],
                           s.wqt[dc * 128:(dc + 1) * 128, qs * 512:(qs + 1) * 512],
                           wait=[("mm", s.ctr["mm"].v)])

                @blk.tensor
                def _(t, dc=dc, ld_w=ld_w):
                    t.wait_ge(s.sem["d_sy"], ld_w)
                    if dc == 0:
                        t.wait_ge(s.sem["ve"], s.ctr["ve"].v)
                    for tc in range(TCN):
                        inst = t.matmul(ps8[tc][:],
                                        s.xt_sb[:, dc * 1024 + tc * 128: dc * 1024 + (tc + 1) * 128],
                                        wqs[:], start=(dc == 0), stop=(dc == DCN - 1))
                    inst.then_inc(s.sem["mm"], 1)
                s.ctr["mm"].inc()
            mm_q = s.ctr["mm"].v

            for tc in range(TCN):
                @blk.vector
                def _(v, tc=tc, qs=qs, mm_q=mm_q, ld_bq=ld_bq):
                    v = _eng_wrap(v, s, "ve")
                    v.wait_ge(s.sem["mm"], mm_q)
                    v.wait_ge(s.sem["d_sy"], ld_bq)
                    v.wait_ge(s.sem["se"], s.ctr["se"].v)
                    v.tensor_tensor(tqa[:], ps8[tc][:], bqb[:, qs * 512:(qs + 1) * 512],
                                    ALU.add)
                    v.tensor_scalar(tqb[:], tqa[:], 0.0, None, ALU.min)
                    v.tensor_scalar(tqa[:], tqa[:], 0.0, None,
                                    ALU.max)
                ve_q = s.ctr["ve"].v

                @blk.scalar
                def _(e, ve_q=ve_q):
                    e = _eng_wrap(e, s, "se")
                    e.wait_ge(s.sem["ve"], ve_q)
                    e.activation(tqc[:], tqb[:], AF.Exp, bias=0.0,
                                 scale=1.0)
                se_x = s.ctr["se"].v

                @blk.vector
                def _(v, tc=tc, qs=qs, se_x=se_x):
                    v = _eng_wrap(v, s, "ve")
                    v.wait_ge(s.sem["se"], se_x)
                    v.tensor_tensor(sig[:, tc * 1024 + qs * 512: tc * 1024 + (qs + 1) * 512],
                                    tqa[:], tqc[:], ALU.add)

        for tc in range(TCN):
            for k in range(8):
                @blk.tensor
                def _(t, tc=tc, k=k):
                    t.wait_ge(s.sem["ve"], s.ctr["ve"].v)
                    t.transpose(psb, sig[:, tc * 1024 + k * 128: tc * 1024 + (k + 1) * 128],
                                s.ident_sb[:]).then_inc(s.sem["mm"], 1)
                mm_tr = s.ctr["mm"].inc()

                @blk.vector
                def _(v, tc=tc, k=k, mm_tr=mm_tr):
                    v = _eng_wrap(v, s, "ve")
                    v.wait_ge(s.sem["mm"], mm_tr)
                    v.tensor_copy(sigt[:, k * 1024 + tc * 128: k * 1024 + (tc + 1) * 128],
                                  psb)

        for k in range(8):
            @blk.tensor
            def _(t, k=k):
                t.wait_ge(s.sem["ve"], s.ctr["ve"].v)
                t.matmul(ps8[6][0:1, :], s.zb_sb[:, k:k + 1],
                         sigt[:, k * 1024: k * 1024 + 512], start=True, stop=True)
                t.matmul(ps8[7][0:1, :], s.zb_sb[:, k:k + 1],
                         sigt[:, k * 1024 + 512: (k + 1) * 1024],
                         start=True, stop=True).then_inc(s.sem["mm"], 1)
            mm_den = s.ctr["mm"].inc()

            @blk.vector
            def _(v, mm_den=mm_den):
                v = _eng_wrap(v, s, "ve")
                v.wait_ge(s.sem["mm"], mm_den)
                v.tensor_copy(drow[0:1, 0:512], ps8[6][0:1, :])
                v.tensor_copy(drow[0:1, 512:1024], ps8[7][0:1, :])
                v.tensor_scalar(drow[0:1, :], drow[0:1, :], float(EPS_B), None,
                                ALU.add)
            ve_d0 = s.ctr["ve"].v

            @blk.vector
            def _(v, ve_d0=ve_d0):
                v = _eng_wrap(v, s, "ve")
                v.wait_ge(s.sem["ve"], ve_d0)
                v.reciprocal(drow[0:1, :], drow[0:1, :])
            ve_d = s.ctr["ve"].v

            st_row = dma(blk, s, "sy", s.rowd.ap(), drow[0:1, :], wait=[("ve", ve_d)])
            ld_row = dma(blk, s, "sy", rden[:],
                         s.rowd.ap().broadcast_to((128, 1024)),
                         wait=[("d_sy", st_row)])

            @blk.tensor
            def _(t, k=k, ld_row=ld_row):
                t.wait_ge(s.sem["d_sy"], ld_row)
                t.wait_ge(s.sem["ve"], s.ctr["ve"].v)
                t.matmul(ps8[0][:], s.mb_sb[:, k * 128:(k + 1) * 128],
                         sigt[:, k * 1024: k * 1024 + 512], start=True, stop=True)
                t.matmul(ps8[1][:], s.mb_sb[:, k * 128:(k + 1) * 128],
                         sigt[:, k * 1024 + 512: (k + 1) * 1024],
                         start=True, stop=True).then_inc(s.sem["mm"], 1)
            mm_num = s.ctr["mm"].inc()

            @blk.vector
            def _(v, k=k, mm_num=mm_num, ld_row=ld_row):
                v = _eng_wrap(v, s, "ve")
                v.wait_ge(s.sem["mm"], mm_num)
                v.wait_ge(s.sem["d_sy"], ld_row)
                v.tensor_tensor(retb[:, k * 1024: k * 1024 + 512], ps8[0][:],
                                rden[:, 0:512], ALU.mult)
                v.tensor_tensor(retb[:, k * 1024 + 512: (k + 1) * 1024], ps8[1][:],
                                rden[:, 512:1024], ALU.mult)

        mm_l = s.ctr["mm"].v
        for ds in range(8):
            for k in range(8):
                ld_w = dma(blk, s, "sy", wof[:],
                           s.woft[k * 128:(k + 1) * 128, ds * 512:(ds + 1) * 512],
                           wait=[("mm", mm_l)])

                @blk.tensor
                def _(t, k=k, ld_w=ld_w):
                    t.wait_ge(s.sem["d_sy"], ld_w)
                    if k == 0:
                        t.wait_ge(s.sem["ve"], s.ctr["ve"].v)
                    for tc in range(TCN):
                        inst = t.matmul(ps8[tc][:],
                                        retb[:, k * 1024 + tc * 128: k * 1024 + (tc + 1) * 128],
                                        wof[:], start=(k == 0), stop=(k == 7))
                    inst.then_inc(s.sem["mm"], 1)
                mm_l = s.ctr["mm"].inc()

            lds = []
            for tc in range(TCN):
                lds.append(dma(blk, s, "sc", xsl[:, tc * 512:(tc + 1) * 512],
                               src_ap[tc * 128:(tc + 1) * 128, ds * 512:(ds + 1) * 512],
                               wait=[("d_sy", s.ctr["d_sy"].v)] if tc == 0 else ()))

            @blk.vector
            def _(v, mm_l=mm_l, lds=lds):
                v = _eng_wrap(v, s, "ve")
                v.wait_ge(s.sem["mm"], mm_l)
                v.wait_ge(s.sem["d_sc"], lds[-1])
                for tc in range(TCN):
                    inst = v.tensor_tensor(xsl[:, tc * 512:(tc + 1) * 512],
                                           xsl[:, tc * 512:(tc + 1) * 512],
                                           ps8[tc][:], ALU.add)
                inst
            ve_up = s.ctr["ve"].v

            for tc in range(TCN):
                dma(blk, s, "sy", dst_ap[tc * 128:(tc + 1) * 128, ds * 512:(ds + 1) * 512],
                    xsl[:, tc * 512:(tc + 1) * 512],
                    wait=[("ve", ve_up)] if tc == 0 else ())


def build_nc():
    ct = const_tables()
    nc = bass.Bass("TRN2", target_bir_lowering=False, debug=False,
                   detect_race_conditions=RACECHECK)
    s = St()

    s.in_x = nc.dram_tensor("x_in", [T, D], F32, kind="ExternalInput").ap()
    s.in_w1q = nc.dram_tensor("w1q", [HSL, D], I32, kind="ExternalInput").ap()
    s.in_w1s = nc.dram_tensor("w1s", [HSL, 64], F32, kind="ExternalInput").ap()
    s.in_w2q = nc.dram_tensor("w2q", [HSL, D], I32, kind="ExternalInput").ap()
    s.in_w2s = nc.dram_tensor("w2s", [HSL, 64], F32, kind="ExternalInput").ap()
    s.in_w3q = nc.dram_tensor("w3q", [D, HSL], I32, kind="ExternalInput").ap()
    s.in_w3s = nc.dram_tensor("w3s", [D, 22], F32, kind="ExternalInput").ap()
    s.in_wq = nc.dram_tensor("wq", [1024, D], F32, kind="ExternalInput").ap()
    s.in_bq = nc.dram_tensor("bq", [1024], F32, kind="ExternalInput").ap()
    s.in_wo = nc.dram_tensor("wo", [D, D], F32, kind="ExternalInput").ap()
    s.in_m = nc.dram_tensor("m_in", [8, 128, 128], F32, kind="ExternalInput").ap()
    s.in_z = nc.dram_tensor("z_in", [8, 128], F32, kind="ExternalInput").ap()
    s.in_tabA = nc.dram_tensor("tabA", list(ct["tabA"].shape), BF,
                               kind="ExternalInput").ap()
    s.in_tabD = nc.dram_tensor("tabD", list(ct["tabD"].shape), BF,
                               kind="ExternalInput").ap()
    s.in_ident = nc.dram_tensor("identc", [128, 128], BF, kind="ExternalInput").ap()
    s.out_y = nc.dram_tensor("y", [T, D], F32, kind="ExternalOutput").ap()
    if DEBUG:
        s.dbg = {
            "dbg_w1t": nc.dram_tensor("dbg_w1t", [256, HSL], BF, kind="ExternalOutput").ap(),
            "dbg_w3t": nc.dram_tensor("dbg_w3t", [256, D], BF, kind="ExternalOutput").ap(),
            "dbg_wqt": nc.dram_tensor("dbg_wqt", [256, 1024], BF, kind="ExternalOutput").ap(),
            "dbg_woft": nc.dram_tensor("dbg_woft", [256, D], BF, kind="ExternalOutput").ap(),
            "dbg_lnx": nc.dram_tensor("dbg_lnx", [256, T], BF, kind="ExternalOutput").ap(),
            "dbg_hp": nc.dram_tensor("dbg_hp", [HIDP, T], BF, kind="ExternalOutput").ap(),
            "dbg_re": nc.dram_tensor("dbg_re", [128, NFC * 256], BF, kind="ExternalOutput").ap(),
            "dbg_im": nc.dram_tensor("dbg_im", [128, NFC * 256], BF, kind="ExternalOutput").ap(),
            "dbg_pb": nc.dram_tensor("dbg_pb", [128, NFC * 256], BF, kind="ExternalOutput").ap(),
            "dbg_corr": nc.dram_tensor("dbg_corr", [128, 4096], F32, kind="ExternalOutput").ap(),
            "dbg_t8": nc.dram_tensor("dbg_t8", [128, 8], F32, kind="ExternalOutput").ap(),
            "dbg_g": nc.dram_tensor("dbg_g", [128, 4096], BF, kind="ExternalOutput").ap(),
            "dbg_tre": nc.dram_tensor("dbg_tre", [128, NFC * 256], BF, kind="ExternalOutput").ap(),
            "dbg_tim": nc.dram_tensor("dbg_tim", [128, NFC * 256], BF, kind="ExternalOutput").ap(),
            "dbg_rz": nc.dram_tensor("dbg_rz", [128, 24], F32, kind="ExternalOutput").ap(),
            "dbg_xt": nc.dram_tensor("dbg_xt", [128, 2048], BF, kind="ExternalOutput").ap(),
        }

    s.xs = nc.dram_tensor("xs", [T, D], F32).ap()
    s.lnx = nc.dram_tensor("lnx", [D, T], BF).ap()
    s.hprime = nc.dram_tensor("hprime", [HIDP, T], BF)
    s.hprime = s.hprime.ap()
    s.w1t_my_h = nc.dram_tensor("w1t_my", [D, HSL], BF)
    s.w1t_g_h = nc.dram_tensor("w1t_g", [NCORES * D, HSL], BF)
    s.w2t_my_h = nc.dram_tensor("w2t_my", [D, HSL], BF)
    s.w2t_g_h = nc.dram_tensor("w2t_g", [NCORES * D, HSL], BF)
    s.w3t_my_h = nc.dram_tensor("w3t_my", [HSL, D], BF)
    s.w3t_g_h = nc.dram_tensor("w3t_g", [HIDP, D], BF)
    s.w1t_my, s.w1t_g = s.w1t_my_h.ap(), s.w1t_g_h.ap()
    s.w2t_my, s.w2t_g = s.w2t_my_h.ap(), s.w2t_g_h.ap()
    s.w3t_my, s.w3t_g = s.w3t_my_h.ap(), s.w3t_g_h.ap()
    s.wqt = nc.dram_tensor("wqt", [D, 1024], BF).ap()
    s.woft = nc.dram_tensor("woft", [1024, D], BF).ap()
    s.rowd = nc.dram_tensor("rowd", [1, 1024], F32)

    with contextlib.ExitStack() as outer:
        s.ident_sb = outer.enter_context(nc.sbuf_tensor(uname("identsb"), [128, 128], BF))
        s.mb_sb = outer.enter_context(nc.sbuf_tensor(uname("mbsb"), [128, 1024], BF))
        s.zb_sb = outer.enter_context(nc.sbuf_tensor(uname("zbsb"), [128, 8], BF))
        s.lncoef = outer.enter_context(nc.sbuf_tensor(uname("lncoef"), [128, 16], F32))
        s.rz8 = outer.enter_context(nc.sbuf_tensor(uname("rz8"), [128, 8], F32))
        s.sem = {}
        for k in SEM_NAMES:
            s.sem[k] = outer.enter_context(nc.semaphore(k))
        s.ctr = {k: Ctr() for k in SEM_NAMES}
        blk = outer.enter_context(nc.Block())

        class _W:
            pass

        p_prep(nc, blk, s)
        barrier(blk, s)
        if DEBUG:
            dma(blk, s, "sy", s.dbg["dbg_w1t"], s.w1t_g[4096:4352, :])
            dma(blk, s, "sy", s.dbg["dbg_w3t"], s.w3t_g[1408:1664, :])
            dma(blk, s, "sy", s.dbg["dbg_wqt"], s.wqt[0:256, :])
            dma(blk, s, "sy", s.dbg["dbg_woft"], s.woft[0:256, :])

        if N_SUB >= 1:
            p_ln(nc, blk, s, s.in_x, to_dram=True, save_coefs=True)
            barrier(blk, s)
            if DEBUG:
                dma(blk, s, "sy", s.dbg["dbg_lnx"], s.lnx[0:256, :])
            if ATTNA:
                p_attn_a(nc, blk, s, s.in_x, s.xs)
                if AA_STAGE < 4 or AA_NQ < 4:
                    barrier(blk, s)
                    r0 = 0 if AA_STAGE < 4 else AA_NQ * 256
                    if r0 < T:
                        dma(blk, s, "sy", s.xs[r0:T, :], s.in_x[r0:T, :])
            else:
                dma(blk, s, "sy", s.xs, s.in_x)
            barrier(blk, s)
            cur = s.xs
        else:
            cur = s.in_x

        if N_SUB >= 2:
            with contextlib.ExitStack() as fx:
                s.xt_sb = fx.enter_context(nc.sbuf_tensor(uname("xtsb1"), [128, DCN * 1024], BF))
                p_ln(nc, blk, s, cur, to_dram=False)
                barrier(blk, s)
                if DEBUG:
                    dma(blk, s, "sy", s.dbg["dbg_xt"], s.xt_sb[:, 0:2048])
                dst = s.out_y if N_SUB == 2 else s.xs
                p_ffn(nc, blk, s, cur, dst)
                barrier(blk, s)
                cur = dst

        if N_SUB >= 3:
            with contextlib.ExitStack() as bx:
                s.xt_sb = bx.enter_context(nc.sbuf_tensor(uname("xtsb2"), [128, DCN * 1024], BF))
                p_ln(nc, blk, s, cur, to_dram=False)
                barrier(blk, s)
                dst = s.out_y if N_SUB == 3 else s.xs
                p_attn_b(nc, blk, s, cur, dst)
                barrier(blk, s)
                cur = dst

        if N_SUB >= 4:
            with contextlib.ExitStack() as fx:
                s.xt_sb = fx.enter_context(nc.sbuf_tensor(uname("xtsb3"), [128, DCN * 1024], BF))
                p_ln(nc, blk, s, cur, to_dram=False)
                barrier(blk, s)
                p_ffn(nc, blk, s, cur, s.out_y)
                barrier(blk, s)

        if DEBUG:
            dma(blk, s, "sy", s.dbg["dbg_rz"][:, 0:8], s.rz8[:])
            dma(blk, s, "sy", s.dbg["dbg_rz"][:, 8:24], s.lncoef[:])
            if N_SUB < 1:
                dma(blk, s, "sy", s.dbg["dbg_lnx"], s.lnx[0:256, :])
            dma(blk, s, "sy", s.dbg["dbg_hp"], s.hprime[:, :])
            barrier(blk, s)

        if N_SUB == 1:
            cpm = dma(blk, s, "sy", s.out_y, s.xs)

            @blk.sync
            def _(e):
                e.wait_ge(s.sem["d_sy"], s.ctr["d_sy"].v)

        if N_SUB == 0:
            cpm = dma(blk, s, "sy", s.out_y, s.in_x)

            @blk.sync
            def _(e):
                e.wait_ge(s.sem["d_sy"], s.ctr["d_sy"].v)

        barrier(blk, s)
    return nc


_NC_CACHE = {}


def make_in_maps(inputs):
    _dead = (inputs.get("L0_attn_b_Wk"), inputs.get("L0_attn_b_bk"),
             inputs.get("L0_attn_b_Wv"), inputs.get("L0_attn_b_bv"))
    ct = const_tables()
    x = np.asarray(inputs["L0_attn_a_x"], np.float32).reshape(8192, D)

    def pad_rows(a, rows):
        out = np.zeros((rows,) + a.shape[1:], a.dtype)
        out[:a.shape[0]] = a
        return out

    def pad_cols(a, cols):
        out = np.zeros(a.shape[:1] + (cols,) + a.shape[2:], a.dtype)
        out[:, :a.shape[1]] = a
        return out

    w1q = pad_rows(np.asarray(inputs["L0_ffn_w1_q"], np.int32), HIDP)
    w1s = pad_rows(np.asarray(inputs["L0_ffn_w1_scale"], np.float32), HIDP)
    w2q = pad_rows(np.asarray(inputs["L0_ffn_w2_q"], np.int32), HIDP)
    w2s = pad_rows(np.asarray(inputs["L0_ffn_w2_scale"], np.float32), HIDP)
    w3q = pad_cols(np.asarray(inputs["L0_ffn_w3_q"], np.int32), HIDP)
    w3s = pad_cols(np.asarray(inputs["L0_ffn_w3_scale"], np.float32), HIDP // 64)

    wq_full = np.asarray(inputs["L0_attn_b_Wq"], np.float32)
    bq_full = np.asarray(inputs["L0_attn_b_bq"], np.float32)
    sel = np.concatenate([np.arange(k * 512, k * 512 + 128) for k in range(8)])
    wq_sel = np.ascontiguousarray(wq_full[sel])
    bq_sel = np.ascontiguousarray(bq_full[sel])
    wo = np.ascontiguousarray(np.asarray(inputs["L0_attn_b_Wo"], np.float32))
    m_in = np.ascontiguousarray(np.asarray(inputs["L0_attn_b_M"], np.float32))
    z_in = np.ascontiguousarray(np.asarray(inputs["L0_attn_b_z"], np.float32))

    in_maps = []
    for c in range(NCORES):
        in_maps.append({
            "x_in": np.ascontiguousarray(x[c * T:(c + 1) * T]),
            "w1q": np.ascontiguousarray(w1q[c * HSL:(c + 1) * HSL]),
            "w1s": np.ascontiguousarray(w1s[c * HSL:(c + 1) * HSL]),
            "w2q": np.ascontiguousarray(w2q[c * HSL:(c + 1) * HSL]),
            "w2s": np.ascontiguousarray(w2s[c * HSL:(c + 1) * HSL]),
            "w3q": np.ascontiguousarray(w3q[:, c * HSL:(c + 1) * HSL]),
            "w3s": np.ascontiguousarray(w3s[:, c * (HSL // 64):(c + 1) * (HSL // 64)]),
            "wq": wq_sel, "bq": bq_sel, "wo": wo, "m_in": m_in, "z_in": z_in,
            "tabA": ct["tabA"], "tabD": ct["tabD"], "identc": ct["ident"],
        })
    return in_maps


def kernel(**inputs):
    in_maps = make_in_maps(inputs)
    if "nc" not in _NC_CACHE:
        _NC_CACHE["nc"] = build_nc()
    nc = _NC_CACHE["nc"]
    res = run_bass_kernel_spmd(nc, in_maps, core_ids=list(range(NCORES)))
    out = np.concatenate([np.asarray(r["y"], np.float32) for r in res.results], axis=0)
    return out.reshape(4, 2048, 4096)


# version 03 (final)
# speedup vs baseline: 1.0517x; 1.0517x over previous
import contextlib
import numpy as np
import ml_dtypes
import concourse.bass as bass
import concourse.mybir as mybir
from concourse.bass_utils import run_bass_kernel_spmd

BF16 = ml_dtypes.bfloat16
F32 = mybir.dt.float32
BF = mybir.dt.bfloat16
I32 = mybir.dt.int32
AF = mybir.ActivationFunctionType
ALU = mybir.AluOpType
AX = mybir.AxisListType

NCORES = 8
T = 1024
D = 4096
L = 4096
NF = 2049
FP = 2176
NFC = FP // 128
HID = 11008
HIDP = 11264
HCH = HIDP // 128
HSL = HIDP // NCORES
HCSL = HSL // 128
DCN = D // 128
TCN = T // 128
EPS_LN = 1e-5
EPS_B = 1e-6

NF4 = [-1.0, -0.6961928009986877, -0.5250730514526367, -0.39491748809814453,
       -0.28444138169288635, -0.18477343022823334, -0.09105003625154495, 0.0,
       0.07958029955625534, 0.16093020141124725, 0.24611230194568634,
       0.33791524171829224, 0.44070982933044434, 0.5626170039176941,
       0.7229568362236023, 1.0]
NF4_T0 = float(np.float32(NF4[0]))
NF4_D = [float(np.float32(np.float32(NF4[v]) - np.float32(NF4[v - 1])))
         for v in range(1, 16)]

N_SUB = 4
DEBUG = False
ATTNA = True
AA_STAGE = 4
AA_NQ = 4
AA_NOTAILMM = False
PREP = True
RACECHECK = True

_CONST = {}


def const_tables():
    if _CONST:
        return _CONST
    j = np.arange(L, dtype=np.int64)[:, None]
    f = np.arange(FP, dtype=np.int64)[None, :]
    th = 2.0 * np.pi * ((j * f) % L) / L
    cf = np.cos(th)
    sf = np.sin(th)
    cf[:, NF:] = 0.0
    sf[:, NF:] = 0.0
    tabA = np.concatenate([cf, sf], axis=0).astype(BF16)
    w = np.full(FP, 2.0 / L)
    w[0] = 1.0 / L
    w[L // 2] = 1.0 / L
    w[NF:] = 0.0
    dcos = (cf * w[None, :]).T.copy()
    dsin = (sf * w[None, :]).T.copy()
    tabD = np.concatenate([dcos, dsin], axis=0).astype(BF16)
    ident = np.eye(128, dtype=BF16)
    _CONST.update(tabA=tabA, tabD=tabD, ident=ident)
    return _CONST


class Ctr:
    __slots__ = ("v",)

    def __init__(self):
        self.v = 0

    def inc(self, n=1):
        self.v += n
        return self.v


_UNIQ = [0]


def uname(base):
    _UNIQ[0] += 1
    return f"{base}_{_UNIQ[0]}"


class St:
    pass


SEM_NAMES = ("d_sy", "d_sc", "d_gp", "mm", "ve", "se", "cc")

_PASSTHRU = {"wait_ge", "sem_inc", "dma_start", "wait_op"}


def _eng_wrap(eng, s, semname):
    sem = s.sem[semname]
    ctr = s.ctr[semname]

    class _W:
        def __getattr__(self, name):
            fn = getattr(eng, name)
            if name in _PASSTHRU or not callable(fn):
                return fn

            def wrap(*a, **k):
                eng.wait_ge(sem, ctr.v)
                inst = fn(*a, **k)
                inst.then_inc(sem, 1)
                ctr.inc()
                return inst
            return wrap
    return _W()


def barrier(blk, s):
    vals = [(s.sem[k], s.ctr[k].v) for k in SEM_NAMES if s.ctr[k].v > 0]
    for en in ("sync", "scalar", "vector", "tensor", "gpsimd"):
        dec = getattr(blk, en)

        @dec
        def _(eng, vals=vals):
            for sem, v in vals:
                eng.wait_ge(sem, v)


def dma(blk, s, queue, out, in_, wait=(), **kw):
    sem = s.sem["d_" + queue]
    ctr = s.ctr["d_" + queue]
    eng = {"sy": blk.sync, "sc": blk.scalar, "gp": blk.gpsimd}[queue]

    @eng
    def _(e):
        for sn, v in wait:
            if v > 0:
                e.wait_ge(s.sem[sn], v)
        e.dma_start(out=out, in_=in_, **kw).then_inc(sem, 16)
    return ctr.inc(16)


def p_prep(nc, blk, s):
    ld_id = dma(blk, s, "sy", s.ident_sb[:], s.in_ident)
    s.ld_id = ld_id

    with contextlib.ExitStack() as cx:
        mtmp = cx.enter_context(nc.sbuf_tensor(uname("pp_mtmp"), [128, 1024], F32))
        ztmp = cx.enter_context(nc.sbuf_tensor(uname("pp_ztmp"), [128, 8], F32))
        for k in range(8):
            ld_m = dma(blk, s, "sy", mtmp[:, k * 128:(k + 1) * 128], s.in_m[k])
        for k in range(8):
            ld_z = dma(blk, s, "sy", ztmp[:, k:k + 1], s.in_z[k:k + 1, :].rearrange("o d -> d o"))

        @blk.vector
        def _(v):
            v = _eng_wrap(v, s, "ve")
            v.wait_ge(s.sem["d_sy"], ld_z)
            v.tensor_copy(s.mb_sb[:], mtmp[:])
            v.tensor_copy(s.zb_sb[:], ztmp[:])

        if not PREP:
            return
        wtile = cx.enter_context(nc.sbuf_tensor(uname("pp_wtile"), [128, 4096], F32))
        wtb = cx.enter_context(nc.sbuf_tensor(uname("pp_wtb"), [128, 4096], BF))
        strip = cx.enter_context(nc.sbuf_tensor(uname("pp_strip"), [128, 4096], BF))
        pt = cx.enter_context(nc.psum_tensor(uname("pp_pt"), [128, 512], F32))
        ptb = pt[:, 0:64].bitcast(BF)

        for qc in range(8):
            ld = dma(blk, s, "sc", wtile[:], s.in_wq[qc * 128:(qc + 1) * 128, :],
                     wait=[("mm", s.ctr["mm"].v)])

            @blk.vector
            def _(v, ld=ld):
                v = _eng_wrap(v, s, "ve")
                v.wait_ge(s.sem["d_sc"], ld)
                v.tensor_copy(wtb[:], wtile[:])
            ve_cast = s.ctr["ve"].v

            for dc in range(DCN):
                @blk.tensor
                def _(t, dc=dc, ve_cast=ve_cast):
                    t.wait_ge(s.sem["ve"], ve_cast)
                    t.wait_ge(s.sem["d_sy"], s.ld_id)
                    t.wait_ge(s.sem["ve"], s.ctr["ve"].v)
                    t.transpose(ptb, wtb[:, dc * 128:(dc + 1) * 128],
                                s.ident_sb[:]).then_inc(s.sem["mm"], 1)
                mm_tr = s.ctr["mm"].inc()

                @blk.vector
                def _(v, dc=dc, mm_tr=mm_tr):
                    v = _eng_wrap(v, s, "ve")
                    v.wait_ge(s.sem["mm"], mm_tr)
                    v.tensor_copy(strip[:, dc * 128:(dc + 1) * 128],
                                  ptb)
            dma(blk, s, "sy",
                s.wqt[:, qc * 128:(qc + 1) * 128].rearrange("(dc p) q -> p dc q", p=128),
                strip[:].rearrange("p (dc q) -> p dc q", q=128),
                wait=[("ve", s.ctr["ve"].v)])

        for dc in range(DCN):
            for k in range(8):
                ld = dma(blk, s, "sc", wtile[:, 0:512],
                         s.in_wo[dc * 128:(dc + 1) * 128, k * 512:(k + 1) * 512],
                         wait=[("mm", s.ctr["mm"].v)])

                @blk.vector
                def _(v, ld=ld):
                    v = _eng_wrap(v, s, "ve")
                    v.wait_ge(s.sem["d_sc"], ld)
                    v.wait_ge(s.sem["d_sy"], s.ctr["d_sy"].v)
                    v.tensor_tensor(wtile[:, 512:640], wtile[:, 0:128],
                                    wtile[:, 128:256], ALU.add)
                    v.tensor_tensor(wtile[:, 640:768], wtile[:, 256:384],
                                    wtile[:, 384:512], ALU.add)
                    v.tensor_tensor(wtile[:, 512:640], wtile[:, 512:640],
                                    wtile[:, 640:768], ALU.add)
                    v.tensor_copy(wtb[:, 0:128], wtile[:, 512:640])
                ve_fold = s.ctr["ve"].v

                @blk.tensor
                def _(t, ve_fold=ve_fold):
                    t.wait_ge(s.sem["ve"], ve_fold)
                    t.wait_ge(s.sem["d_sy"], s.ld_id)
                    t.transpose(ptb, wtb[:, 0:128], s.ident_sb[:]).then_inc(s.sem["mm"], 1)
                mm_tr = s.ctr["mm"].inc()

                @blk.vector
                def _(v, mm_tr=mm_tr):
                    v = _eng_wrap(v, s, "ve")
                    v.wait_ge(s.sem["mm"], mm_tr)
                    v.tensor_copy(strip[:, 0:128], ptb)
                dma(blk, s, "sy",
                    s.woft[k * 128:(k + 1) * 128, dc * 128:(dc + 1) * 128],
                    strip[:, 0:128], wait=[("ve", s.ctr["ve"].v)])

        qtile = cx.enter_context(nc.sbuf_tensor(uname("dq_q"), [128, 4096], I32))
        qf = cx.enter_context(nc.sbuf_tensor(uname("dq_qf"), [128, 4096], F32))
        acc = cx.enter_context(nc.sbuf_tensor(uname("dq_acc"), [128, 4096], F32))
        tmp = cx.enter_context(nc.sbuf_tensor(uname("dq_tmp"), [128, 4096], F32))
        wbb = cx.enter_context(nc.sbuf_tensor(uname("dq_wb"), [128, 4096], BF))
        sct = cx.enter_context(nc.sbuf_tensor(uname("dq_sc"), [128, 64], F32))

        def dequant_into_wb(qsrc, scsrc, nblk, width):
            ld_q = dma(blk, s, "sc", qtile[:, 0:width], qsrc,
                       wait=[("ve", s.ctr["ve"].v)])
            ld_s = dma(blk, s, "sc", sct[:, 0:nblk], scsrc)

            @blk.vector
            def _(v):
                v = _eng_wrap(v, s, "ve")
                v.wait_ge(s.sem["d_sc"], ld_s)
                v.wait_ge(s.sem["mm"], s.ctr["mm"].v)
                v.tensor_copy(qf[:, 0:width], qtile[:, 0:width])
                v.tensor_scalar(acc[:, 0:width], qf[:, 0:width], 1.0, NF4_D[0],
                                ALU.is_ge, ALU.mult)
                for vv in range(2, 16):
                    v.tensor_scalar(tmp[:, 0:width], qf[:, 0:width], float(vv),
                                    NF4_D[vv - 1], ALU.is_ge, ALU.mult)
                    v.tensor_tensor(acc[:, 0:width], acc[:, 0:width],
                                    tmp[:, 0:width], ALU.add)
                for b in range(nblk):
                    v.tensor_scalar(wbb[:, b * 64:(b + 1) * 64],
                                    acc[:, b * 64:(b + 1) * 64],
                                    NF4_T0, sct[:, b:b + 1], ALU.add, ALU.mult)
                v.tensor_copy(tmp[:, 0:1], acc[:, 0:1])
            return s.ctr["ve"].v

        def transpose_store(ve_w, nblocks, dst_ap_fn):
            for bi in range(nblocks):
                @blk.tensor
                def _(t, bi=bi, ve_w=ve_w):
                    t.wait_ge(s.sem["ve"], ve_w)
                    t.wait_ge(s.sem["d_sy"], s.ld_id)
                    t.wait_ge(s.sem["ve"], s.ctr["ve"].v)
                    if bi == 0:
                        t.wait_ge(s.sem["d_sy"], s.ctr["d_sy"].v)
                    t.transpose(ptb, wbb[:, bi * 128:(bi + 1) * 128],
                                s.ident_sb[:]).then_inc(s.sem["mm"], 1)
                mm_tr = s.ctr["mm"].inc()

                @blk.vector
                def _(v, bi=bi, mm_tr=mm_tr):
                    v = _eng_wrap(v, s, "ve")
                    v.wait_ge(s.sem["mm"], mm_tr)
                    v.tensor_copy(strip[:, bi * 128:(bi + 1) * 128],
                                  ptb)
            dram_ap, sbuf_ap = dst_ap_fn()
            dma(blk, s, "sy", dram_ap, sbuf_ap, wait=[("ve", s.ctr["ve"].v)])

        for (qarr, scarr, dstT) in ((s.in_w1q, s.in_w1s, s.w1t_my),
                                    (s.in_w2q, s.in_w2s, s.w2t_my)):
            for hl in range(HCSL):
                ve_w = dequant_into_wb(qarr[hl * 128:(hl + 1) * 128, :],
                                       scarr[hl * 128:(hl + 1) * 128, :], 64, 4096)
                transpose_store(
                    ve_w, DCN,
                    lambda dstT=dstT, hl=hl: (
                        dstT[:, hl * 128:(hl + 1) * 128].rearrange(
                            "(dc p) h -> p dc h", p=128),
                        strip[:].rearrange("p (dc h) -> p dc h", h=128)))

        for dc3 in range(DCN):
            ve_w = dequant_into_wb(s.in_w3q[dc3 * 128:(dc3 + 1) * 128, :],
                                   s.in_w3s[dc3 * 128:(dc3 + 1) * 128, :], 22, HSL)
            transpose_store(
                ve_w, HCSL,
                lambda dc3=dc3: (
                    s.w3t_my[:, dc3 * 128:(dc3 + 1) * 128].rearrange(
                        "(hl p) d -> p hl d", p=128),
                    strip[:, 0:HSL].rearrange("p (hl d) -> p hl d", d=128)))

    @blk.gpsimd
    def _(g):
        g.wait_ge(s.sem["d_sy"], s.ctr["d_sy"].v)
        g.wait_ge(s.sem["d_sc"], s.ctr["d_sc"].v)
        for piece, full in ((s.w1t_my_h, s.w1t_g_h), (s.w2t_my_h, s.w2t_g_h),
                            (s.w3t_my_h, s.w3t_g_h)):
            g.collective_compute(
                "AllGather", ALU.bypass,
                replica_groups=[list(range(NCORES))],
                ins=[piece.ap().opt()],
                outs=[full.ap().opt()],
            ).then_inc(s.sem["cc"])
    s.ctr["cc"].inc(3)


def p_ln(nc, blk, s, src_ap, to_dram, save_coefs=False):
    with contextlib.ExitStack() as cx:
        xt = cx.enter_context(nc.sbuf_tensor(uname("ln_x"), [128, 4096], F32))
        dum = cx.enter_context(nc.sbuf_tensor(uname("ln_dum"), [128, 4096], BF))
        xn = cx.enter_context(nc.sbuf_tensor(uname("ln_xn"), [128, 4096], BF))
        strip = cx.enter_context(nc.sbuf_tensor(uname("ln_strip"), [128, 4096], BF))
        cols = cx.enter_context(nc.sbuf_tensor(uname("ln_cols"), [128, 6], F32))
        ptl = cx.enter_context(nc.psum_tensor(uname("ln_pt"), [128, 512], F32))
        ptlb = ptl[:, 0:64].bitcast(BF)
        scol, qcol, mcol, vcol, rcol, bcol = (cols[:, i:i + 1] for i in range(6))

        for tc in range(TCN):
            ld = dma(blk, s, "sc", xt[:], src_ap[tc * 128:(tc + 1) * 128, :],
                     wait=[("ve", s.ctr["ve"].v), ("se", s.ctr["se"].v)])

            @blk.vector
            def _(v, ld=ld):
                v = _eng_wrap(v, s, "ve")
                v.wait_ge(s.sem["d_sc"], ld)
                v.reduce_sum(scol, xt[:], axis=AX.X)
            ve_sum = s.ctr["ve"].v

            @blk.scalar
            def _(e, ld=ld):
                e = _eng_wrap(e, s, "se")
                e.wait_ge(s.sem["d_sc"], ld)
                e.activation(dum[:], xt[:], AF.Square, bias=0.0, scale=1.0,
                             accum_out=qcol)
            se_sq = s.ctr["se"].v

            @blk.vector
            def _(v, se_sq=se_sq, ve_sum=ve_sum):
                v = _eng_wrap(v, s, "ve")
                v.wait_ge(s.sem["se"], se_sq)
                v.wait_ge(s.sem["ve"], ve_sum)
                v.tensor_scalar(mcol, scol, 1.0 / D, None, ALU.mult)
                v.tensor_scalar(vcol, qcol, 1.0 / D, None, ALU.mult)
                v.tensor_tensor(bcol, mcol, mcol, ALU.mult)
                v.tensor_tensor(vcol, vcol, bcol, ALU.subtract)
                v.tensor_scalar(vcol, vcol, float(EPS_LN), None,
                                ALU.add)
            ve_var = s.ctr["ve"].v

            @blk.scalar
            def _(e, ve_var=ve_var):
                e = _eng_wrap(e, s, "se")
                e.wait_ge(s.sem["ve"], ve_var)
                e.activation(rcol, vcol, AF.Sqrt, bias=0.0, scale=1.0)
            se_rt = s.ctr["se"].v

            @blk.vector
            def _(v, se_rt=se_rt):
                v = _eng_wrap(v, s, "ve")
                v.wait_ge(s.sem["se"], se_rt)
                v.wait_ge(s.sem["mm"], s.ctr["mm"].v)
                v.reciprocal(rcol, rcol)
            ve_rc = s.ctr["ve"].v

            @blk.vector
            def _(v, tc=tc, ve_rc=ve_rc):
                v = _eng_wrap(v, s, "ve")
                v.wait_ge(s.sem["ve"], ve_rc)
                v.tensor_tensor(bcol, mcol, rcol, ALU.mult)
                v.tensor_scalar(bcol, bcol, -1.0, None, ALU.mult)
                if save_coefs:
                    v.tensor_copy(s.lncoef[:, 2 * tc:2 * tc + 1], rcol)
                    v.tensor_copy(s.lncoef[:, 2 * tc + 1:2 * tc + 2], bcol)
                v.tensor_copy(qcol, bcol)
            ve_bc = s.ctr["ve"].v

            @blk.vector
            def _(v, ve_bc=ve_bc):
                v = _eng_wrap(v, s, "ve")
                v.wait_ge(s.sem["ve"], ve_bc)
                v.tensor_scalar(xn[:], xt[:], rcol, bcol, ALU.mult,
                                ALU.add)
            ve_xn = s.ctr["ve"].v

            for dc in range(DCN):
                @blk.tensor
                def _(t, dc=dc, ve_xn=ve_xn):
                    t.wait_ge(s.sem["ve"], ve_xn)
                    t.wait_ge(s.sem["ve"], s.ctr["ve"].v)
                    t.wait_ge(s.sem["d_sy"], s.ld_id)
                    if dc == 0:
                        t.wait_ge(s.sem["d_sy"], s.ctr["d_sy"].v)
                    t.transpose(ptlb, xn[:, dc * 128:(dc + 1) * 128],
                                s.ident_sb[:]).then_inc(s.sem["mm"], 1)
                mm_tr = s.ctr["mm"].inc()

                @blk.vector
                def _(v, dc=dc, tc=tc, mm_tr=mm_tr):
                    v = _eng_wrap(v, s, "ve")
                    v.wait_ge(s.sem["mm"], mm_tr)
                    if to_dram:
                        v.tensor_copy(strip[:, dc * 128:(dc + 1) * 128],
                                      ptlb)
                    else:
                        v.tensor_copy(
                            s.xt_sb[:, dc * 1024 + tc * 128: dc * 1024 + (tc + 1) * 128],
                            ptlb)

            if to_dram:
                dma(blk, s, "sy",
                    s.lnx[:, tc * 128:(tc + 1) * 128].rearrange("(dc p) t -> p dc t", p=128),
                    strip[:].rearrange("p (dc t) -> p dc t", t=128),
                    wait=[("ve", s.ctr["ve"].v)])


def p_attn_a(nc, blk, s, src_ap, dst_ap):
    QW = 256
    NQ = T // QW
    TPQ = QW // 128
    for qq in range(min(NQ, AA_NQ)):
        with contextlib.ExitStack() as cx:
            lnh = cx.enter_context(nc.sbuf_tensor(uname("aa_lnh"), [128, DCN * QW], BF))
            reb = cx.enter_context(nc.sbuf_tensor(uname("aa_re"), [128, NFC * QW], BF))
            imb = cx.enter_context(nc.sbuf_tensor(uname("aa_im"), [128, NFC * QW], BF))
            pbf = cx.enter_context(nc.sbuf_tensor(uname("aa_p"), [128, NFC * QW], BF))
            slab = cx.enter_context(nc.sbuf_tensor(uname("aa_slab"), [128, 4096], BF))
            slab2 = cx.enter_context(nc.sbuf_tensor(uname("aa_slab2"), [128, 4096], BF))
            corr = cx.enter_context(nc.sbuf_tensor(uname("aa_corr"), [128, 4096], F32))
            ebuf = cx.enter_context(nc.sbuf_tensor(uname("aa_e"), [128, 4096], F32))
            mbuf = cx.enter_context(nc.sbuf_tensor(uname("aa_m"), [128, 4096], F32))
            gbuf = cx.enter_context(nc.sbuf_tensor(uname("aa_g"), [128, 4096], BF))
            gt = cx.enter_context(nc.sbuf_tensor(uname("aa_gt"), [128, DCN * QW], BF))
            tre = cx.enter_context(nc.sbuf_tensor(uname("aa_tre"), [128, NFC * QW], BF))
            tim = cx.enter_context(nc.sbuf_tensor(uname("aa_tim"), [128, NFC * QW], BF))
            gc2 = cx.enter_context(nc.sbuf_tensor(uname("aa_gc"), [128, QW], BF))
            gs2 = cx.enter_context(nc.sbuf_tensor(uname("aa_gs"), [128, QW], BF))
            tt1 = cx.enter_context(nc.sbuf_tensor(uname("aa_tt1"), [128, QW], F32))
            tt2 = cx.enter_context(nc.sbuf_tensor(uname("aa_tt2"), [128, QW], F32))
            t8 = cx.enter_context(nc.sbuf_tensor(uname("aa_t8"), [128, 8], F32))
            e8 = cx.enter_context(nc.sbuf_tensor(uname("aa_e8"), [128, 8], F32))
            ccols = cx.enter_context(nc.sbuf_tensor(uname("aa_cols"), [128, 4], F32))
            xtl = cx.enter_context(nc.sbuf_tensor(uname("aa_x"), [128, 4096], F32))
            ps8 = [cx.enter_context(nc.psum_tensor(uname(f"aa_ps{i}"), [128, 512], F32))
                   for i in range(8)]
            psb = ps8[0][:, 0:64].bitcast(BF)
            nmx, zc, c1c, c2c = (ccols[:, i:i + 1] for i in range(4))

            ld_ln = dma(blk, s, "sc",
                        lnh[:].rearrange("p (dc t) -> p dc t", t=QW),
                        s.lnx[:, qq * QW:(qq + 1) * QW].rearrange(
                            "(dc p) t -> p dc t", p=128))

            for fc in range(NFC):
                w_mm = [("mm", s.ctr["mm"].v)]
                ld_c = dma(blk, s, "sy",
                           slab[:].rearrange("p (jc f) -> p jc f", f=128),
                           s.in_tabA[0:4096, fc * 128:(fc + 1) * 128].rearrange(
                               "(jc p) f -> p jc f", p=128), wait=w_mm)
                ld_s2 = dma(blk, s, "sy",
                            slab2[:].rearrange("p (jc f) -> p jc f", f=128),
                            s.in_tabA[4096:8192, fc * 128:(fc + 1) * 128].rearrange(
                                "(jc p) f -> p jc f", p=128))

                @blk.tensor
                def _(t, fc=fc, ld_s2=ld_s2, ld_ln=ld_ln):
                    t.wait_ge(s.sem["d_sy"], ld_s2)
                    t.wait_ge(s.sem["d_sc"], ld_ln)
                    t.wait_ge(s.sem["ve"], s.ctr["ve"].v)
                    for dc in range(DCN):
                        t.matmul(ps8[0][:, 0:QW], slab[:, dc * 128:(dc + 1) * 128],
                                 lnh[:, dc * QW:(dc + 1) * QW],
                                 start=(dc == 0), stop=(dc == DCN - 1))
                    for dc in range(DCN):
                        inst = t.matmul(ps8[1][:, 0:QW], slab2[:, dc * 128:(dc + 1) * 128],
                                        lnh[:, dc * QW:(dc + 1) * QW],
                                        start=(dc == 0), stop=(dc == DCN - 1))
                    inst.then_inc(s.sem["mm"], 1)
                mm_f = s.ctr["mm"].inc()

                @blk.vector
                def _(v, fc=fc, mm_f=mm_f):
                    v = _eng_wrap(v, s, "ve")
                    v.wait_ge(s.sem["mm"], mm_f)
                    sl = slice(fc * QW, (fc + 1) * QW)
                    v.tensor_copy(reb[:, sl], ps8[0][:, 0:QW])
                    v.tensor_copy(imb[:, sl], ps8[1][:, 0:QW])
                    v.tensor_tensor(tt1[:], reb[:, sl], reb[:, sl], ALU.mult)
                    v.tensor_tensor(tt2[:], imb[:, sl], imb[:, sl], ALU.mult)
                    v.tensor_tensor(pbf[:, sl], tt1[:], tt2[:],
                                    ALU.add)

            if DEBUG and qq == 0:
                barrier(blk, s)
                dma(blk, s, "sy", s.dbg["dbg_re"], reb[:])
                dma(blk, s, "sy", s.dbg["dbg_im"], imb[:])
                dma(blk, s, "sy", s.dbg["dbg_pb"], pbf[:])

            for ti in range(TPQ if AA_STAGE >= 2 else 0):
                tc = qq * TPQ + ti
                for fc in range(NFC):
                    ld_d = dma(blk, s, "sy", slab[:],
                               s.in_tabD[fc * 128:(fc + 1) * 128, :],
                               wait=[("mm", s.ctr["mm"].v)])

                    @blk.tensor
                    def _(t, fc=fc, ti=ti, ld_d=ld_d):
                        t.wait_ge(s.sem["d_sy"], ld_d)
                        if fc == 0:
                            t.wait_ge(s.sem["ve"], s.ctr["ve"].v)
                        for ts_ in range(8):
                            inst = t.matmul(ps8[ts_][:],
                                            pbf[:, fc * QW + ti * 128: fc * QW + (ti + 1) * 128],
                                            slab[:, ts_ * 512:(ts_ + 1) * 512],
                                            start=(fc == 0), stop=(fc == NFC - 1))
                        inst.then_inc(s.sem["mm"], 1)
                    s.ctr["mm"].inc()
                mm_corr = s.ctr["mm"].v

                @blk.vector
                def _(v, mm_corr=mm_corr):
                    v = _eng_wrap(v, s, "ve")
                    v.wait_ge(s.sem["mm"], mm_corr)
                    for ts_ in range(8):
                        inst = v.tensor_copy(corr[:, ts_ * 512:(ts_ + 1) * 512],
                                             ps8[ts_][:])
                    inst
                ve_cp = s.ctr["ve"].v

                @blk.vector
                def _(v, ve_cp=ve_cp):
                    v = _eng_wrap(v, s, "ve")
                    v.wait_ge(s.sem["ve"], ve_cp)
                    v.max(t8[:], corr[:])
                ve_mx = s.ctr["ve"].v

                @blk.vector
                def _(v, ve_mx=ve_mx):
                    v = _eng_wrap(v, s, "ve")
                    v.wait_ge(s.sem["ve"], ve_mx)
                    v.tensor_scalar(nmx, t8[:, 0:1], -1.0, None,
                                    ALU.mult)
                ve_c = s.ctr["ve"].v

                @blk.scalar
                def _(e, ve_c=ve_c):
                    e = _eng_wrap(e, s, "se")
                    e.wait_ge(s.sem["ve"], ve_c)
                    e.activation(e8[:], t8[:], AF.Exp, bias=nmx, scale=1.0)
                    e.activation(ebuf[:], corr[:], AF.Exp, bias=nmx,
                                 scale=1.0)
                se_e = s.ctr["se"].v

                @blk.vector
                def _(v, se_e=se_e):
                    v = _eng_wrap(v, s, "ve")
                    v.wait_ge(s.sem["se"], se_e)
                    v.reduce_sum(zc, e8[:], axis=AX.X)
                ve_rs = s.ctr["ve"].v

                @blk.vector
                def _(v, ve_rs=ve_rs):
                    v = _eng_wrap(v, s, "ve")
                    v.wait_ge(s.sem["ve"], ve_rs)
                    v.reciprocal(zc, zc)
                ve_rz = s.ctr["ve"].v

                @blk.vector
                def _(v, tc=tc, ve_rz=ve_rz):
                    v = _eng_wrap(v, s, "ve")
                    v.wait_ge(s.sem["ve"], ve_rz)
                    v.tensor_copy(s.rz8[:, tc:tc + 1], zc)
                    v.tensor_scalar(mbuf[:], corr[:], t8[:, 7:8], None, ALU.is_ge)
                    v.tensor_tensor(ebuf[:], mbuf[:], ebuf[:], ALU.mult)
                    v.tensor_scalar(ebuf[:, 0:1], ebuf[:, 0:1], 0.0, None, ALU.mult)
                    v.tensor_scalar(gbuf[:], ebuf[:], zc, None,
                                    ALU.mult)
                ve_g = s.ctr["ve"].v

                for dcq in range(DCN):
                    @blk.tensor
                    def _(t, dcq=dcq, ve_g=ve_g):
                        t.wait_ge(s.sem["ve"], ve_g)
                        t.wait_ge(s.sem["ve"], s.ctr["ve"].v)
                        t.transpose(psb, gbuf[:, dcq * 128:(dcq + 1) * 128],
                                    s.ident_sb[:]).then_inc(s.sem["mm"], 1)
                    mm_tr = s.ctr["mm"].inc()

                    @blk.vector
                    def _(v, dcq=dcq, ti=ti, mm_tr=mm_tr):
                        v = _eng_wrap(v, s, "ve")
                        v.wait_ge(s.sem["mm"], mm_tr)
                        v.tensor_copy(gt[:, dcq * QW + ti * 128: dcq * QW + (ti + 1) * 128],
                                      psb)

            if DEBUG and qq == 0 and AA_STAGE >= 2:
                barrier(blk, s)
                dma(blk, s, "sy", s.dbg["dbg_corr"], corr[:])
                dma(blk, s, "sy", s.dbg["dbg_t8"], t8[:])
                dma(blk, s, "sy", s.dbg["dbg_g"], gbuf[:])

            for fc in range(NFC if AA_STAGE >= 3 else 0):
                w_mm = [("mm", s.ctr["mm"].v)]
                ld_c = dma(blk, s, "sy",
                           slab[:].rearrange("p (jc f) -> p jc f", f=128),
                           s.in_tabA[0:4096, fc * 128:(fc + 1) * 128].rearrange(
                               "(jc p) f -> p jc f", p=128), wait=w_mm)
                ld_s2 = dma(blk, s, "sy",
                            slab2[:].rearrange("p (jc f) -> p jc f", f=128),
                            s.in_tabA[4096:8192, fc * 128:(fc + 1) * 128].rearrange(
                                "(jc p) f -> p jc f", p=128))

                @blk.tensor
                def _(t, fc=fc, ld_s2=ld_s2):
                    t.wait_ge(s.sem["d_sy"], ld_s2)
                    t.wait_ge(s.sem["ve"], s.ctr["ve"].v)
                    for tq in range(DCN):
                        t.matmul(ps8[0][:, 0:QW], slab[:, tq * 128:(tq + 1) * 128],
                                 gt[:, tq * QW:(tq + 1) * QW],
                                 start=(tq == 0), stop=(tq == DCN - 1))
                    for tq in range(DCN):
                        inst = t.matmul(ps8[1][:, 0:QW], slab2[:, tq * 128:(tq + 1) * 128],
                                        gt[:, tq * QW:(tq + 1) * QW],
                                        start=(tq == 0), stop=(tq == DCN - 1))
                    inst.then_inc(s.sem["mm"], 1)
                mm_g = s.ctr["mm"].inc()

                @blk.vector
                def _(v, fc=fc, mm_g=mm_g):
                    v = _eng_wrap(v, s, "ve")
                    v.wait_ge(s.sem["mm"], mm_g)
                    sl = slice(fc * QW, (fc + 1) * QW)
                    v.tensor_copy(gc2[:], ps8[0][:, 0:QW])
                    v.tensor_copy(gs2[:], ps8[1][:, 0:QW])
                    v.tensor_tensor(tt1[:], reb[:, sl], gc2[:], ALU.mult)
                    v.tensor_tensor(tt2[:], imb[:, sl], gs2[:], ALU.mult)
                    v.tensor_tensor(tre[:, sl], tt1[:], tt2[:], ALU.add)
                    v.tensor_tensor(tt1[:], imb[:, sl], gc2[:], ALU.mult)
                    v.tensor_tensor(tt2[:], reb[:, sl], gs2[:], ALU.mult)
                    v.tensor_tensor(tim[:, sl], tt1[:], tt2[:],
                                    ALU.subtract)

            if DEBUG and qq == 0 and AA_STAGE >= 3:
                barrier(blk, s)
                dma(blk, s, "sy", s.dbg["dbg_tre"], tre[:])
                dma(blk, s, "sy", s.dbg["dbg_tim"], tim[:])

            for ti in range(TPQ if AA_STAGE >= 4 else 0):
                tc = qq * TPQ + ti
                ld_x = dma(blk, s, "sc", xtl[:], src_ap[tc * 128:(tc + 1) * 128, :],
                           wait=[("d_sy", s.ctr["d_sy"].v)])

                @blk.vector
                def _(v, tc=tc, qq=qq, ti=ti):
                    v = _eng_wrap(v, s, "ve")
                    v.tensor_tensor(c1c, s.lncoef[:, 2 * tc:2 * tc + 1],
                                    s.rz8[:, tc:tc + 1], ALU.mult)
                    v.tensor_scalar(c1c, c1c, 1.0, None, ALU.add)
                    v.tensor_tensor(c2c, s.lncoef[:, 2 * tc + 1:2 * tc + 2],
                                    s.rz8[:, tc:tc + 1], ALU.mult)
                    if DEBUG and qq == 0 and ti == 0:
                        v.tensor_copy(s.rz8[:, 6:7], c1c)
                        v.tensor_copy(s.rz8[:, 7:8], c2c)
                    v.tensor_copy(c2c, c2c)
                ve_cc = s.ctr["ve"].v

                @blk.vector
                def _(v, ld_x=ld_x, ve_cc=ve_cc):
                    v = _eng_wrap(v, s, "ve")
                    v.wait_ge(s.sem["d_sc"], ld_x)
                    v.wait_ge(s.sem["ve"], ve_cc)
                    v.tensor_scalar(xtl[:], xtl[:], c1c, None, ALU.mult)
                    v.tensor_scalar(xtl[:], xtl[:], c2c, None,
                                    ALU.add)

                for fc in range(NFC if not AA_NOTAILMM else 0):
                    w_mm = [("mm", s.ctr["mm"].v)]
                    ld_dc = dma(blk, s, "sy", slab[:],
                                s.in_tabD[fc * 128:(fc + 1) * 128, :], wait=w_mm)
                    ld_ds = dma(blk, s, "sy", slab2[:],
                                s.in_tabD[FP + fc * 128: FP + (fc + 1) * 128, :])

                    @blk.tensor
                    def _(t, fc=fc, ti=ti, ld_ds=ld_ds):
                        t.wait_ge(s.sem["d_sy"], ld_ds)
                        if fc == 0:
                            t.wait_ge(s.sem["ve"], s.ctr["ve"].v)
                        for ds in range(8):
                            t.matmul(ps8[ds][:],
                                     tre[:, fc * QW + ti * 128: fc * QW + (ti + 1) * 128],
                                     slab[:, ds * 512:(ds + 1) * 512],
                                     start=(fc == 0), stop=False)
                            inst = t.matmul(ps8[ds][:],
                                            tim[:, fc * QW + ti * 128: fc * QW + (ti + 1) * 128],
                                            slab2[:, ds * 512:(ds + 1) * 512],
                                            start=False, stop=(fc == NFC - 1))
                        inst.then_inc(s.sem["mm"], 1)
                    s.ctr["mm"].inc()
                mm_tail = s.ctr["mm"].v

                @blk.vector
                def _(v, mm_tail=mm_tail):
                    v = _eng_wrap(v, s, "ve")
                    v.wait_ge(s.sem["mm"], mm_tail)
                    if AA_NOTAILMM:
                        inst = v.tensor_copy(c1c, c1c)
                    else:
                        for ds in range(8):
                            inst = v.tensor_tensor(xtl[:, ds * 512:(ds + 1) * 512],
                                                   xtl[:, ds * 512:(ds + 1) * 512],
                                                   ps8[ds][:], ALU.add)
                    inst
                ve_up = s.ctr["ve"].v

                dma(blk, s, "sy", dst_ap[tc * 128:(tc + 1) * 128, :], xtl[:],
                    wait=[("ve", ve_up)])


def p_ffn(nc, blk, s, src_ap, dst_ap):
    h_done = [0] * HCH
    with contextlib.ExitStack() as cx:
        w1s = [cx.enter_context(nc.sbuf_tensor(uname(f"f1_w1{i}"), [128, 4096], BF))
               for i in range(2)]
        w2s = [cx.enter_context(nc.sbuf_tensor(uname(f"f1_w2{i}"), [128, 4096], BF))
               for i in range(2)]
        sg = cx.enter_context(nc.sbuf_tensor(uname("f1_sg"), [128, 1024], F32))
        hb = [cx.enter_context(nc.sbuf_tensor(uname(f"f1_hb{i}"), [128, 1024], BF))
              for i in range(2)]
        ps8 = [cx.enter_context(nc.psum_tensor(uname(f"f1_ps{i}"), [128, 512], F32))
               for i in range(8)]
        mm_hist = [0, 0]
        ve_hist = [0, 0]
        hstore = [0, 0]

        for hc in range(HCH):
            par = hc % 2
            r, hl = hc // HCSL, hc % HCSL
            w_w = [("mm", mm_hist[par])]
            if hc < 2:
                w_w.append(("cc", s.ctr["cc"].v))
            ld1 = dma(blk, s, "sy",
                      w1s[par][:].rearrange("p (dc h) -> p dc h", h=128),
                      s.w1t_g[r * 4096:(r + 1) * 4096, hl * 128:(hl + 1) * 128]
                      .rearrange("(dc p) h -> p dc h", p=128), wait=w_w)
            ld2 = dma(blk, s, "sy",
                      w2s[par][:].rearrange("p (dc h) -> p dc h", h=128),
                      s.w2t_g[r * 4096:(r + 1) * 4096, hl * 128:(hl + 1) * 128]
                      .rearrange("(dc p) h -> p dc h", p=128))

            @blk.tensor
            def _(t, par=par, ld2=ld2):
                t.wait_ge(s.sem["d_sy"], ld2)
                t.wait_ge(s.sem["ve"], ve_hist[par])
                for dc in range(DCN):
                    for (pg, wsl, toff) in ((0, w1s[par], 0), (1, w1s[par], 512),
                                            (2, w2s[par], 0), (3, w2s[par], 512)):
                        inst = t.matmul(ps8[par * 4 + pg][:],
                                        wsl[:, dc * 128:(dc + 1) * 128],
                                        s.xt_sb[:, dc * 1024 + toff: dc * 1024 + toff + 512],
                                        start=(dc == 0), stop=(dc == DCN - 1))
                inst.then_inc(s.sem["mm"], 1)
            mm_hist[par] = s.ctr["mm"].inc()

            @blk.scalar
            def _(e, par=par, mmv=mm_hist[par]):
                e = _eng_wrap(e, s, "se")
                e.wait_ge(s.sem["mm"], mmv)
                e.activation(sg[:, 0:512], ps8[par * 4 + 0][:], AF.Silu,
                             bias=0.0, scale=1.0)
                e.activation(sg[:, 512:1024], ps8[par * 4 + 1][:], AF.Silu,
                             bias=0.0, scale=1.0)
            se_si = s.ctr["se"].v

            @blk.vector
            def _(v, par=par, se_si=se_si, hs=hstore[par]):
                v = _eng_wrap(v, s, "ve")
                v.wait_ge(s.sem["se"], se_si)
                v.wait_ge(s.sem["d_gp"], hs)
                v.tensor_tensor(hb[par][:, 0:512], sg[:, 0:512],
                                ps8[par * 4 + 2][:], ALU.mult)
                v.tensor_tensor(hb[par][:, 512:1024], sg[:, 512:1024],
                                ps8[par * 4 + 3][:], ALU.mult)
            ve_hist[par] = s.ctr["ve"].v

            h_done[hc] = dma(blk, s, "gp", s.hprime[hc * 128:(hc + 1) * 128, :],
                             hb[par][:], wait=[("ve", ve_hist[par])])
            hstore[par] = h_done[hc]

    barrier(blk, s)

    NBUF2 = 4
    with contextlib.ExitStack() as cx:
        w3b = [cx.enter_context(nc.sbuf_tensor(uname(f"f2_w3{i}"), [128, 512], BF))
               for i in range(NBUF2)]
        hcb = [cx.enter_context(nc.sbuf_tensor(uname(f"f2_hc{i}"), [128, 1024], BF))
               for i in range(NBUF2)]
        xsl = cx.enter_context(nc.sbuf_tensor(uname("f2_xs"), [128, 8 * 512], F32))
        ps8 = [cx.enter_context(nc.psum_tensor(uname(f"f2_ps{i}"), [128, 512], F32))
               for i in range(8)]

        mm_last = [0] * NBUF2
        for ds in range(8):
            for hc in range(HCH):
                par = hc % NBUF2
                w_w = [("mm", mm_last[par])]
                ld3 = dma(blk, s, "sy", w3b[par][:],
                          s.w3t_g[hc * 128:(hc + 1) * 128, ds * 512:(ds + 1) * 512],
                          wait=w_w)
                ldh = dma(blk, s, "sc", hcb[par][:],
                          s.hprime[hc * 128:(hc + 1) * 128, :], wait=w_w)

                @blk.tensor
                def _(t, par=par, hc=hc, ld3=ld3, ldh=ldh):
                    t.wait_ge(s.sem["d_sy"], ld3)
                    t.wait_ge(s.sem["d_sc"], ldh)
                    if hc == 0:
                        t.wait_ge(s.sem["ve"], s.ctr["ve"].v)
                    for tc in range(TCN):
                        inst = t.matmul(ps8[tc][:], hcb[par][:, tc * 128:(tc + 1) * 128],
                                        w3b[par][:],
                                        start=(hc == 0), stop=(hc == HCH - 1))
                    inst.then_inc(s.sem["mm"], 1)
                mm_last[par] = s.ctr["mm"].inc()
            mm_ds = s.ctr["mm"].v

            lds = []
            for tc in range(TCN):
                lds.append(dma(blk, s, "sc", xsl[:, tc * 512:(tc + 1) * 512],
                               src_ap[tc * 128:(tc + 1) * 128, ds * 512:(ds + 1) * 512],
                               wait=[("d_sy", s.ctr["d_sy"].v)] if tc == 0 else ()))

            @blk.vector
            def _(v, mm_ds=mm_ds, lds=lds):
                v = _eng_wrap(v, s, "ve")
                v.wait_ge(s.sem["mm"], mm_ds)
                v.wait_ge(s.sem["d_sc"], lds[-1])
                for tc in range(TCN):
                    inst = v.tensor_tensor(xsl[:, tc * 512:(tc + 1) * 512],
                                           xsl[:, tc * 512:(tc + 1) * 512],
                                           ps8[tc][:], ALU.add)
                inst
            ve_up = s.ctr["ve"].v

            for tc in range(TCN):
                dma(blk, s, "sy", dst_ap[tc * 128:(tc + 1) * 128, ds * 512:(ds + 1) * 512],
                    xsl[:, tc * 512:(tc + 1) * 512],
                    wait=[("ve", ve_up)] if tc == 0 else ())


def p_attn_b(nc, blk, s, src_ap, dst_ap):
    with contextlib.ExitStack() as cx:
        bqb = cx.enter_context(nc.sbuf_tensor(uname("ab_bq"), [128, 1024], F32))
        wqs = cx.enter_context(nc.sbuf_tensor(uname("ab_wqs"), [128, 512], BF))
        sig = cx.enter_context(nc.sbuf_tensor(uname("ab_sig"), [128, 8 * 1024], BF))
        sigt = cx.enter_context(nc.sbuf_tensor(uname("ab_sigt"), [128, 8 * 1024], BF))
        retb = cx.enter_context(nc.sbuf_tensor(uname("ab_ret"), [128, 8 * 1024], BF))
        rden = cx.enter_context(nc.sbuf_tensor(uname("ab_rden"), [128, 1024], F32))
        drow = cx.enter_context(nc.sbuf_tensor(uname("ab_drow"), [1, 1024], F32))
        tqa = cx.enter_context(nc.sbuf_tensor(uname("ab_tqa"), [128, 512], F32))
        tqb = cx.enter_context(nc.sbuf_tensor(uname("ab_tqb"), [128, 512], F32))
        tqc = cx.enter_context(nc.sbuf_tensor(uname("ab_tqc"), [128, 512], F32))
        wof = cx.enter_context(nc.sbuf_tensor(uname("ab_wof"), [128, 512], BF))
        xsl = cx.enter_context(nc.sbuf_tensor(uname("ab_xs"), [128, 8 * 512], F32))
        ps8 = [cx.enter_context(nc.psum_tensor(uname(f"ab_ps{i}"), [128, 512], F32))
               for i in range(8)]
        psb = ps8[0][:, 0:64].bitcast(BF)

        ld_bq = dma(blk, s, "sy", bqb[:],
                    s.in_bq.unsqueeze(0).broadcast_to((128, 1024)))

        for qs in range(2):
            for dc in range(DCN):
                ld_w = dma(blk, s, "sy", wqs[:],
                           s.wqt[dc * 128:(dc + 1) * 128, qs * 512:(qs + 1) * 512],
                           wait=[("mm", s.ctr["mm"].v)])

                @blk.tensor
                def _(t, dc=dc, ld_w=ld_w):
                    t.wait_ge(s.sem["d_sy"], ld_w)
                    if dc == 0:
                        t.wait_ge(s.sem["ve"], s.ctr["ve"].v)
                    for tc in range(TCN):
                        inst = t.matmul(ps8[tc][:],
                                        s.xt_sb[:, dc * 1024 + tc * 128: dc * 1024 + (tc + 1) * 128],
                                        wqs[:], start=(dc == 0), stop=(dc == DCN - 1))
                    inst.then_inc(s.sem["mm"], 1)
                s.ctr["mm"].inc()
            mm_q = s.ctr["mm"].v

            for tc in range(TCN):
                @blk.vector
                def _(v, tc=tc, qs=qs, mm_q=mm_q, ld_bq=ld_bq):
                    v = _eng_wrap(v, s, "ve")
                    v.wait_ge(s.sem["mm"], mm_q)
                    v.wait_ge(s.sem["d_sy"], ld_bq)
                    v.wait_ge(s.sem["se"], s.ctr["se"].v)
                    v.tensor_tensor(tqa[:], ps8[tc][:], bqb[:, qs * 512:(qs + 1) * 512],
                                    ALU.add)
                    v.tensor_scalar(tqb[:], tqa[:], 0.0, None, ALU.min)
                    v.tensor_scalar(tqa[:], tqa[:], 0.0, None,
                                    ALU.max)
                ve_q = s.ctr["ve"].v

                @blk.scalar
                def _(e, ve_q=ve_q):
                    e = _eng_wrap(e, s, "se")
                    e.wait_ge(s.sem["ve"], ve_q)
                    e.activation(tqc[:], tqb[:], AF.Exp, bias=0.0,
                                 scale=1.0)
                se_x = s.ctr["se"].v

                @blk.vector
                def _(v, tc=tc, qs=qs, se_x=se_x):
                    v = _eng_wrap(v, s, "ve")
                    v.wait_ge(s.sem["se"], se_x)
                    v.tensor_tensor(sig[:, tc * 1024 + qs * 512: tc * 1024 + (qs + 1) * 512],
                                    tqa[:], tqc[:], ALU.add)

        for tc in range(TCN):
            for k in range(8):
                @blk.tensor
                def _(t, tc=tc, k=k):
                    t.wait_ge(s.sem["ve"], s.ctr["ve"].v)
                    t.transpose(psb, sig[:, tc * 1024 + k * 128: tc * 1024 + (k + 1) * 128],
                                s.ident_sb[:]).then_inc(s.sem["mm"], 1)
                mm_tr = s.ctr["mm"].inc()

                @blk.vector
                def _(v, tc=tc, k=k, mm_tr=mm_tr):
                    v = _eng_wrap(v, s, "ve")
                    v.wait_ge(s.sem["mm"], mm_tr)
                    v.tensor_copy(sigt[:, k * 1024 + tc * 128: k * 1024 + (tc + 1) * 128],
                                  psb)

        for k in range(8):
            @blk.tensor
            def _(t, k=k):
                t.wait_ge(s.sem["ve"], s.ctr["ve"].v)
                t.matmul(ps8[6][0:1, :], s.zb_sb[:, k:k + 1],
                         sigt[:, k * 1024: k * 1024 + 512], start=True, stop=True)
                t.matmul(ps8[7][0:1, :], s.zb_sb[:, k:k + 1],
                         sigt[:, k * 1024 + 512: (k + 1) * 1024],
                         start=True, stop=True).then_inc(s.sem["mm"], 1)
            mm_den = s.ctr["mm"].inc()

            @blk.vector
            def _(v, mm_den=mm_den):
                v = _eng_wrap(v, s, "ve")
                v.wait_ge(s.sem["mm"], mm_den)
                v.tensor_copy(drow[0:1, 0:512], ps8[6][0:1, :])
                v.tensor_copy(drow[0:1, 512:1024], ps8[7][0:1, :])
                v.tensor_scalar(drow[0:1, :], drow[0:1, :], float(EPS_B), None,
                                ALU.add)
            ve_d0 = s.ctr["ve"].v

            @blk.vector
            def _(v, ve_d0=ve_d0):
                v = _eng_wrap(v, s, "ve")
                v.wait_ge(s.sem["ve"], ve_d0)
                v.reciprocal(drow[0:1, :], drow[0:1, :])
            ve_d = s.ctr["ve"].v

            st_row = dma(blk, s, "sy", s.rowd.ap(), drow[0:1, :], wait=[("ve", ve_d)])
            ld_row = dma(blk, s, "sy", rden[:],
                         s.rowd.ap().broadcast_to((128, 1024)),
                         wait=[("d_sy", st_row)])

            @blk.tensor
            def _(t, k=k, ld_row=ld_row):
                t.wait_ge(s.sem["d_sy"], ld_row)
                t.wait_ge(s.sem["ve"], s.ctr["ve"].v)
                t.matmul(ps8[0][:], s.mb_sb[:, k * 128:(k + 1) * 128],
                         sigt[:, k * 1024: k * 1024 + 512], start=True, stop=True)
                t.matmul(ps8[1][:], s.mb_sb[:, k * 128:(k + 1) * 128],
                         sigt[:, k * 1024 + 512: (k + 1) * 1024],
                         start=True, stop=True).then_inc(s.sem["mm"], 1)
            mm_num = s.ctr["mm"].inc()

            @blk.vector
            def _(v, k=k, mm_num=mm_num, ld_row=ld_row):
                v = _eng_wrap(v, s, "ve")
                v.wait_ge(s.sem["mm"], mm_num)
                v.wait_ge(s.sem["d_sy"], ld_row)
                v.tensor_tensor(retb[:, k * 1024: k * 1024 + 512], ps8[0][:],
                                rden[:, 0:512], ALU.mult)
                v.tensor_tensor(retb[:, k * 1024 + 512: (k + 1) * 1024], ps8[1][:],
                                rden[:, 512:1024], ALU.mult)

        mm_l = s.ctr["mm"].v
        for ds in range(8):
            for k in range(8):
                ld_w = dma(blk, s, "sy", wof[:],
                           s.woft[k * 128:(k + 1) * 128, ds * 512:(ds + 1) * 512],
                           wait=[("mm", mm_l)])

                @blk.tensor
                def _(t, k=k, ld_w=ld_w):
                    t.wait_ge(s.sem["d_sy"], ld_w)
                    if k == 0:
                        t.wait_ge(s.sem["ve"], s.ctr["ve"].v)
                    for tc in range(TCN):
                        inst = t.matmul(ps8[tc][:],
                                        retb[:, k * 1024 + tc * 128: k * 1024 + (tc + 1) * 128],
                                        wof[:], start=(k == 0), stop=(k == 7))
                    inst.then_inc(s.sem["mm"], 1)
                mm_l = s.ctr["mm"].inc()

            lds = []
            for tc in range(TCN):
                lds.append(dma(blk, s, "sc", xsl[:, tc * 512:(tc + 1) * 512],
                               src_ap[tc * 128:(tc + 1) * 128, ds * 512:(ds + 1) * 512],
                               wait=[("d_sy", s.ctr["d_sy"].v)] if tc == 0 else ()))

            @blk.vector
            def _(v, mm_l=mm_l, lds=lds):
                v = _eng_wrap(v, s, "ve")
                v.wait_ge(s.sem["mm"], mm_l)
                v.wait_ge(s.sem["d_sc"], lds[-1])
                for tc in range(TCN):
                    inst = v.tensor_tensor(xsl[:, tc * 512:(tc + 1) * 512],
                                           xsl[:, tc * 512:(tc + 1) * 512],
                                           ps8[tc][:], ALU.add)
                inst
            ve_up = s.ctr["ve"].v

            for tc in range(TCN):
                dma(blk, s, "sy", dst_ap[tc * 128:(tc + 1) * 128, ds * 512:(ds + 1) * 512],
                    xsl[:, tc * 512:(tc + 1) * 512],
                    wait=[("ve", ve_up)] if tc == 0 else ())


def build_nc():
    ct = const_tables()
    nc = bass.Bass("TRN2", target_bir_lowering=False, debug=False,
                   detect_race_conditions=RACECHECK)
    s = St()

    s.in_x = nc.dram_tensor("x_in", [T, D], F32, kind="ExternalInput").ap()
    s.in_w1q = nc.dram_tensor("w1q", [HSL, D], I32, kind="ExternalInput").ap()
    s.in_w1s = nc.dram_tensor("w1s", [HSL, 64], F32, kind="ExternalInput").ap()
    s.in_w2q = nc.dram_tensor("w2q", [HSL, D], I32, kind="ExternalInput").ap()
    s.in_w2s = nc.dram_tensor("w2s", [HSL, 64], F32, kind="ExternalInput").ap()
    s.in_w3q = nc.dram_tensor("w3q", [D, HSL], I32, kind="ExternalInput").ap()
    s.in_w3s = nc.dram_tensor("w3s", [D, 22], F32, kind="ExternalInput").ap()
    s.in_wq = nc.dram_tensor("wq", [1024, D], F32, kind="ExternalInput").ap()
    s.in_bq = nc.dram_tensor("bq", [1024], F32, kind="ExternalInput").ap()
    s.in_wo = nc.dram_tensor("wo", [D, D], F32, kind="ExternalInput").ap()
    s.in_m = nc.dram_tensor("m_in", [8, 128, 128], F32, kind="ExternalInput").ap()
    s.in_z = nc.dram_tensor("z_in", [8, 128], F32, kind="ExternalInput").ap()
    s.in_tabA = nc.dram_tensor("tabA", list(ct["tabA"].shape), BF,
                               kind="ExternalInput").ap()
    s.in_tabD = nc.dram_tensor("tabD", list(ct["tabD"].shape), BF,
                               kind="ExternalInput").ap()
    s.in_ident = nc.dram_tensor("identc", [128, 128], BF, kind="ExternalInput").ap()
    s.out_y = nc.dram_tensor("y", [T, D], F32, kind="ExternalOutput").ap()
    if DEBUG:
        s.dbg = {
            "dbg_w1t": nc.dram_tensor("dbg_w1t", [256, HSL], BF, kind="ExternalOutput").ap(),
            "dbg_w3t": nc.dram_tensor("dbg_w3t", [256, D], BF, kind="ExternalOutput").ap(),
            "dbg_wqt": nc.dram_tensor("dbg_wqt", [256, 1024], BF, kind="ExternalOutput").ap(),
            "dbg_woft": nc.dram_tensor("dbg_woft", [256, D], BF, kind="ExternalOutput").ap(),
            "dbg_lnx": nc.dram_tensor("dbg_lnx", [256, T], BF, kind="ExternalOutput").ap(),
            "dbg_hp": nc.dram_tensor("dbg_hp", [HIDP, T], BF, kind="ExternalOutput").ap(),
            "dbg_re": nc.dram_tensor("dbg_re", [128, NFC * 256], BF, kind="ExternalOutput").ap(),
            "dbg_im": nc.dram_tensor("dbg_im", [128, NFC * 256], BF, kind="ExternalOutput").ap(),
            "dbg_pb": nc.dram_tensor("dbg_pb", [128, NFC * 256], BF, kind="ExternalOutput").ap(),
            "dbg_corr": nc.dram_tensor("dbg_corr", [128, 4096], F32, kind="ExternalOutput").ap(),
            "dbg_t8": nc.dram_tensor("dbg_t8", [128, 8], F32, kind="ExternalOutput").ap(),
            "dbg_g": nc.dram_tensor("dbg_g", [128, 4096], BF, kind="ExternalOutput").ap(),
            "dbg_tre": nc.dram_tensor("dbg_tre", [128, NFC * 256], BF, kind="ExternalOutput").ap(),
            "dbg_tim": nc.dram_tensor("dbg_tim", [128, NFC * 256], BF, kind="ExternalOutput").ap(),
            "dbg_rz": nc.dram_tensor("dbg_rz", [128, 24], F32, kind="ExternalOutput").ap(),
            "dbg_xt": nc.dram_tensor("dbg_xt", [128, 2048], BF, kind="ExternalOutput").ap(),
        }

    s.xs = nc.dram_tensor("xs", [T, D], F32).ap()
    s.lnx = nc.dram_tensor("lnx", [D, T], BF).ap()
    s.hprime = nc.dram_tensor("hprime", [HIDP, T], BF)
    s.hprime = s.hprime.ap()
    s.w1t_my_h = nc.dram_tensor("w1t_my", [D, HSL], BF)
    s.w1t_g_h = nc.dram_tensor("w1t_g", [NCORES * D, HSL], BF)
    s.w2t_my_h = nc.dram_tensor("w2t_my", [D, HSL], BF)
    s.w2t_g_h = nc.dram_tensor("w2t_g", [NCORES * D, HSL], BF)
    s.w3t_my_h = nc.dram_tensor("w3t_my", [HSL, D], BF)
    s.w3t_g_h = nc.dram_tensor("w3t_g", [HIDP, D], BF)
    s.w1t_my, s.w1t_g = s.w1t_my_h.ap(), s.w1t_g_h.ap()
    s.w2t_my, s.w2t_g = s.w2t_my_h.ap(), s.w2t_g_h.ap()
    s.w3t_my, s.w3t_g = s.w3t_my_h.ap(), s.w3t_g_h.ap()
    s.wqt = nc.dram_tensor("wqt", [D, 1024], BF).ap()
    s.woft = nc.dram_tensor("woft", [1024, D], BF).ap()
    s.rowd = nc.dram_tensor("rowd", [1, 1024], F32)

    with contextlib.ExitStack() as outer:
        s.ident_sb = outer.enter_context(nc.sbuf_tensor(uname("identsb"), [128, 128], BF))
        s.mb_sb = outer.enter_context(nc.sbuf_tensor(uname("mbsb"), [128, 1024], BF))
        s.zb_sb = outer.enter_context(nc.sbuf_tensor(uname("zbsb"), [128, 8], BF))
        s.lncoef = outer.enter_context(nc.sbuf_tensor(uname("lncoef"), [128, 16], F32))
        s.rz8 = outer.enter_context(nc.sbuf_tensor(uname("rz8"), [128, 8], F32))
        s.sem = {}
        for k in SEM_NAMES:
            s.sem[k] = outer.enter_context(nc.semaphore(k))
        s.ctr = {k: Ctr() for k in SEM_NAMES}
        blk = outer.enter_context(nc.Block())

        class _W:
            pass

        p_prep(nc, blk, s)
        barrier(blk, s)
        if DEBUG:
            dma(blk, s, "sy", s.dbg["dbg_w1t"], s.w1t_g[4096:4352, :])
            dma(blk, s, "sy", s.dbg["dbg_w3t"], s.w3t_g[1408:1664, :])
            dma(blk, s, "sy", s.dbg["dbg_wqt"], s.wqt[0:256, :])
            dma(blk, s, "sy", s.dbg["dbg_woft"], s.woft[0:256, :])

        if N_SUB >= 1:
            p_ln(nc, blk, s, s.in_x, to_dram=True, save_coefs=True)
            barrier(blk, s)
            if DEBUG:
                dma(blk, s, "sy", s.dbg["dbg_lnx"], s.lnx[0:256, :])
            if ATTNA:
                p_attn_a(nc, blk, s, s.in_x, s.xs)
                if AA_STAGE < 4 or AA_NQ < 4:
                    barrier(blk, s)
                    r0 = 0 if AA_STAGE < 4 else AA_NQ * 256
                    if r0 < T:
                        dma(blk, s, "sy", s.xs[r0:T, :], s.in_x[r0:T, :])
            else:
                dma(blk, s, "sy", s.xs, s.in_x)
            barrier(blk, s)
            cur = s.xs
        else:
            cur = s.in_x

        if N_SUB >= 2:
            with contextlib.ExitStack() as fx:
                s.xt_sb = fx.enter_context(nc.sbuf_tensor(uname("xtsb1"), [128, DCN * 1024], BF))
                p_ln(nc, blk, s, cur, to_dram=False)
                barrier(blk, s)
                if DEBUG:
                    dma(blk, s, "sy", s.dbg["dbg_xt"], s.xt_sb[:, 0:2048])
                dst = s.out_y if N_SUB == 2 else s.xs
                p_ffn(nc, blk, s, cur, dst)
                barrier(blk, s)
                cur = dst

        if N_SUB >= 3:
            with contextlib.ExitStack() as bx:
                s.xt_sb = bx.enter_context(nc.sbuf_tensor(uname("xtsb2"), [128, DCN * 1024], BF))
                p_ln(nc, blk, s, cur, to_dram=False)
                barrier(blk, s)
                dst = s.out_y if N_SUB == 3 else s.xs
                p_attn_b(nc, blk, s, cur, dst)
                barrier(blk, s)
                cur = dst

        if N_SUB >= 4:
            with contextlib.ExitStack() as fx:
                s.xt_sb = fx.enter_context(nc.sbuf_tensor(uname("xtsb3"), [128, DCN * 1024], BF))
                p_ln(nc, blk, s, cur, to_dram=False)
                barrier(blk, s)
                p_ffn(nc, blk, s, cur, s.out_y)
                barrier(blk, s)

        if DEBUG:
            dma(blk, s, "sy", s.dbg["dbg_rz"][:, 0:8], s.rz8[:])
            dma(blk, s, "sy", s.dbg["dbg_rz"][:, 8:24], s.lncoef[:])
            if N_SUB < 1:
                dma(blk, s, "sy", s.dbg["dbg_lnx"], s.lnx[0:256, :])
            dma(blk, s, "sy", s.dbg["dbg_hp"], s.hprime[:, :])
            barrier(blk, s)

        if N_SUB == 1:
            cpm = dma(blk, s, "sy", s.out_y, s.xs)

            @blk.sync
            def _(e):
                e.wait_ge(s.sem["d_sy"], s.ctr["d_sy"].v)

        if N_SUB == 0:
            cpm = dma(blk, s, "sy", s.out_y, s.in_x)

            @blk.sync
            def _(e):
                e.wait_ge(s.sem["d_sy"], s.ctr["d_sy"].v)

        barrier(blk, s)
    return nc


_NC_CACHE = {}


def make_in_maps(inputs):
    _dead = (inputs.get("L0_attn_b_Wk"), inputs.get("L0_attn_b_bk"),
             inputs.get("L0_attn_b_Wv"), inputs.get("L0_attn_b_bv"))
    ct = const_tables()
    x = np.asarray(inputs["L0_attn_a_x"], np.float32).reshape(8192, D)

    def pad_rows(a, rows):
        out = np.zeros((rows,) + a.shape[1:], a.dtype)
        out[:a.shape[0]] = a
        return out

    def pad_cols(a, cols):
        out = np.zeros(a.shape[:1] + (cols,) + a.shape[2:], a.dtype)
        out[:, :a.shape[1]] = a
        return out

    w1q = pad_rows(np.asarray(inputs["L0_ffn_w1_q"], np.int32), HIDP)
    w1s = pad_rows(np.asarray(inputs["L0_ffn_w1_scale"], np.float32), HIDP)
    w2q = pad_rows(np.asarray(inputs["L0_ffn_w2_q"], np.int32), HIDP)
    w2s = pad_rows(np.asarray(inputs["L0_ffn_w2_scale"], np.float32), HIDP)
    w3q = pad_cols(np.asarray(inputs["L0_ffn_w3_q"], np.int32), HIDP)
    w3s = pad_cols(np.asarray(inputs["L0_ffn_w3_scale"], np.float32), HIDP // 64)

    wq_full = np.asarray(inputs["L0_attn_b_Wq"], np.float32)
    bq_full = np.asarray(inputs["L0_attn_b_bq"], np.float32)
    sel = np.concatenate([np.arange(k * 512, k * 512 + 128) for k in range(8)])
    wq_sel = np.ascontiguousarray(wq_full[sel])
    bq_sel = np.ascontiguousarray(bq_full[sel])
    wo = np.ascontiguousarray(np.asarray(inputs["L0_attn_b_Wo"], np.float32))
    m_in = np.ascontiguousarray(np.asarray(inputs["L0_attn_b_M"], np.float32))
    z_in = np.ascontiguousarray(np.asarray(inputs["L0_attn_b_z"], np.float32))

    in_maps = []
    for c in range(NCORES):
        in_maps.append({
            "x_in": np.ascontiguousarray(x[c * T:(c + 1) * T]),
            "w1q": np.ascontiguousarray(w1q[c * HSL:(c + 1) * HSL]),
            "w1s": np.ascontiguousarray(w1s[c * HSL:(c + 1) * HSL]),
            "w2q": np.ascontiguousarray(w2q[c * HSL:(c + 1) * HSL]),
            "w2s": np.ascontiguousarray(w2s[c * HSL:(c + 1) * HSL]),
            "w3q": np.ascontiguousarray(w3q[:, c * HSL:(c + 1) * HSL]),
            "w3s": np.ascontiguousarray(w3s[:, c * (HSL // 64):(c + 1) * (HSL // 64)]),
            "wq": wq_sel, "bq": bq_sel, "wo": wo, "m_in": m_in, "z_in": z_in,
            "tabA": ct["tabA"], "tabD": ct["tabD"], "identc": ct["ident"],
        })
    return in_maps


def kernel(**inputs):
    in_maps = make_in_maps(inputs)
    if "nc" not in _NC_CACHE:
        _NC_CACHE["nc"] = build_nc()
    nc = _NC_CACHE["nc"]
    res = run_bass_kernel_spmd(nc, in_maps, core_ids=list(range(NCORES)))
    out = np.concatenate([np.asarray(r["y"], np.float32) for r in res.results], axis=0)
    return out.reshape(4, 2048, 4096)
